# version 11
# baseline (speedup 1.0000x reference)
import numpy as np
import ml_dtypes
from contextlib import ExitStack

import concourse.bass as bass
import concourse.mybir as mybir
from concourse.bass_utils import run_bass_kernel_spmd

F32 = mybir.dt.float32
BF16 = mybir.dt.bfloat16
I32 = mybir.dt.int32
AF = mybir.ActivationFunctionType
ALU = mybir.AluOpType
AX = mybir.AxisListType

S = 8192
E = 2048
H = 16
DH = 128
NLOC = 2048
NTT = 16
INV = float(np.float32(1.0) / np.float32(np.sqrt(np.float32(128.0))))
LN_EPS = 1e-5
NEG = -1.0e30
BIGI = 1.0e9
UCLIP16 = float(np.float32(16.0 * np.float32(1.0 - 1e-6)))

QB = [[r, 7 - r, 8 + r, 15 - r] for r in range(4)]
SLOT_BASE = [0, 16, 48, 96]
TOT_TILES = 160

ENGS = ("sync", "gpsimd", "tensor", "vector", "scalar")


def owner_of(j):
    p = j // 4
    q = j % 4
    r = q if p % 2 == 0 else 3 - q
    return r, p


def tiles_for(p):
    lst = []
    for j in range(4 * p + 3):
        r, pj = owner_of(j)
        for sub in range(4):
            lst.append((r, r * 16 + pj * 4 + sub))
    for sub in range(4):
        lst.append(("diag", 64 + sub))
    return lst


def build_nc(debug=False):
    nc = bass.Bass("TRN2", target_bir_lowering=False)

    xs_d = nc.dram_tensor("xs", [NLOC, E], F32, kind="ExternalInput").ap()
    wq_d = nc.dram_tensor("wq", [E, E], F32, kind="ExternalInput").ap()
    wk_d = nc.dram_tensor("wk", [E, E], F32, kind="ExternalInput").ap()
    wv_d = nc.dram_tensor("wv", [E, E], F32, kind="ExternalInput").ap()
    wo_d = nc.dram_tensor("wo", [E, E], F32, kind="ExternalInput").ap()
    bo_d = nc.dram_tensor("bo", [1, E], F32, kind="ExternalInput").ap()
    gamma_d = nc.dram_tensor("gamma", [1, E], F32, kind="ExternalInput").ap()
    beta_d = nc.dram_tensor("beta", [1, E], F32, kind="ExternalInput").ap()
    bases_d = nc.dram_tensor("bases", [64, 512], F32, kind="ExternalInput").ap()
    slopes_d = nc.dram_tensor("slopes", [64, 512], F32, kind="ExternalInput").ap()
    oscale_d = nc.dram_tensor("oscale", [64, 1], F32, kind="ExternalInput").ap()
    alibik_d = nc.dram_tensor("alibik", [128, H], F32, kind="ExternalInput").ap()
    tilefac_d = nc.dram_tensor("tilefac", [4 * H, 68], F32, kind="ExternalInput").ap()
    tri_d = nc.dram_tensor("tri", [128, 128], BF16, kind="ExternalInput").ap()
    tdiv32_d = nc.dram_tensor("tdiv32", [1, E], BF16, kind="ExternalInput").ap()
    iota64_d = nc.dram_tensor("iota64", [1, 64], F32, kind="ExternalInput").ap()
    tdiv8_d = nc.dram_tensor("tdiv8", [1, 64], F32, kind="ExternalInput").ap()
    iota8_d = nc.dram_tensor("iota8", [1, 8], F32, kind="ExternalInput").ap()
    iota64c_d = nc.dram_tensor("iota64c", [64, 1], F32, kind="ExternalInput").ap()
    identf_d = nc.dram_tensor("identf", [128, 128], F32, kind="ExternalInput").ap()

    out_d = nc.dram_tensor("out", [NLOC, E], F32, kind="ExternalOutput").ap()

    def scrt(name, shape, dtype):
        kind = "ExternalOutput" if debug else "Internal"
        return nc.dram_tensor(name, shape, dtype, kind=kind).ap()

    xbf_d = nc.dram_tensor("xbf", [NLOC, E], BF16, kind="Internal").ap()
    wbf_q = nc.dram_tensor("wbf_q", [E, E], BF16, kind="Internal").ap()
    wbf_k = nc.dram_tensor("wbf_k", [E, E], BF16, kind="Internal").ap()
    wbf_v = nc.dram_tensor("wbf_v", [E, E], BF16, kind="Internal").ap()
    wbf_o = nc.dram_tensor("wbf_o", [E, E], BF16, kind="Internal").ap()
    qT_scr = scrt("qT_scr", [E, NLOC], BF16)
    kT_in = scrt("kT_in", [E, NLOC], BF16)
    v_in = scrt("v_in", [H * NLOC, DH], BF16)
    kT_ag = nc.dram_tensor("kT_ag", [4 * E, NLOC], BF16, kind="Internal").ap()
    v_ag = nc.dram_tensor("v_ag", [4 * H * NLOC, DH], BF16, kind="Internal").ap()
    ctx_hbm = scrt("ctx_hbm", [NLOC, E], BF16)
    h1_spill = scrt("h1_spill", [NLOC, E], F32)
    ssf_hbm = nc.dram_tensor("ssf_hbm", [64, 32], F32, kind="Internal").ap()
    tidx_hbm = nc.dram_tensor("tidx_hbm", [NTT, 128], F32, kind="Internal").ap()
    ar_in_d = nc.dram_tensor("ar_in", [64, 2], F32, kind="Internal").ap()
    ar_out_d = nc.dram_tensor("ar_out", [64, 2], F32, kind="Internal").ap()

    semkeys = (["dh", "ds", "cc", "tn", "vc", "sc", "gc",
                "dsx0", "dsx1", "dsx2", "dsx3", "dsk0", "dsk1", "dsk2", "dsk3",
                "dsv", "dsq", "dso", "sck",
                "dC", "dS2", "dWo", "dxT", "dW0", "dW1", "dQt",
                "dK0", "dK1", "dK2", "dK3", "dKd", "dX", "dCT0", "dCT1", "dHR", "dOH", "dSp", "dTx",
                "dxT0", "dxT1", "dxT2", "dxT3", "dTf0", "dTf1"]
               + [f"dqK{i}" for i in range(2)] + [f"dqV{i}" for i in range(2)]
               + [f"dqQ{i}" for i in range(2)]
               + [f"dqC{p}{i}" for p in range(4) for i in range(2)])
    cnt = {k: 0 for k in semkeys}
    prog = {k: [] for k in ENGS}

    def add(ek, emit, waits=(), inc=None, n=None):
        if n is None:
            n = 16 if (inc is not None and inc.startswith("d")) else 1
        prog[ek].append((tuple(waits), emit, inc, n))
        if inc is not None:
            cnt[inc] += n
            return cnt[inc]
        return None

    def vadd(emit, waits=()):
        return add("vector", emit, tuple(waits) + (("vc", cnt["vc"]),), "vc")

    def sadd(emit, waits=()):
        return add("scalar", emit, tuple(waits) + (("sc", cnt["sc"]),), "sc")

    def gadd(emit, waits=()):
        return add("gpsimd", emit, tuple(waits) + (("gc", cnt["gc"]),), "gc")

    with ExitStack() as stack:
        ee = stack.enter_context
        sem = {k: ee(nc.semaphore(k)) for k in cnt}
        block = ee(nc.Block())

        alibik_sb = ee(nc.sbuf_tensor("alibik_sb", [128, H], F32))
        tf_sb = ee(nc.sbuf_tensor("tf_sb", [128, 2 * 68], F32))
        tri_sb = ee(nc.sbuf_tensor("tri_sb", [128, 128], BF16))
        tdiv32_sb = ee(nc.sbuf_tensor("tdiv32_sb", [128, E], BF16))
        gamma_sb = ee(nc.sbuf_tensor("gamma_sb", [128, E], F32))
        beta_sb = ee(nc.sbuf_tensor("beta_sb", [128, E], F32))
        bo_sb = ee(nc.sbuf_tensor("bo_sb", [128, E], F32))
        ssf_sb = ee(nc.sbuf_tensor("ssf_sb", [128, E], F32))
        iota64_sb = ee(nc.sbuf_tensor("iota64_sb", [128, 64], F32))
        tdiv8_sb = ee(nc.sbuf_tensor("tdiv8_sb", [128, 64], F32))
        iota8_sb = ee(nc.sbuf_tensor("iota8_sb", [128, 8], F32))
        iota64c_sb = ee(nc.sbuf_tensor("iota64c_sb", [64, 1], F32))
        identf_sb = ee(nc.sbuf_tensor("identf_sb", [128, 128], F32))
        tabs_sb = ee(nc.sbuf_tensor("tabs_sb", [64, 1027], F32))
        xsl_sb = ee(nc.sbuf_tensor("xsl_sb", [128, NTT * 32], F32))
        tidx_sb = ee(nc.sbuf_tensor("tidx_sb", [128, NTT], F32))
        accmin_sb = ee(nc.sbuf_tensor("accmin_sb", [128, 64], F32))
        accmax_sb = ee(nc.sbuf_tensor("accmax_sb", [128, 64], F32))
        mm2_sb = ee(nc.sbuf_tensor("mm2_sb", [64, 2], F32))
        x_sb = ee(nc.sbuf_tensor("x_sb", [128, E], F32))
        h1t_sb = ee(nc.sbuf_tensor("h1t_sb", [128, E], F32))
        scra = ee(nc.sbuf_tensor("scra", [128, E], F32))
        scrb = ee(nc.sbuf_tensor("scrb", [128, E], F32))
        st8 = ee(nc.sbuf_tensor("st8", [128, 8], F32))
        st8b = ee(nc.sbuf_tensor("st8b", [128, 8], F32))
        m64a = ee(nc.sbuf_tensor("m64a", [128, 64], F32))
        m64b = ee(nc.sbuf_tensor("m64b", [128, 64], F32))
        m64c = ee(nc.sbuf_tensor("m64c", [128, 64], F32))
        ts64_sb = ee(nc.sbuf_tensor("ts64_sb", [128, 64], F32))
        i32_sb = ee(nc.sbuf_tensor("i32_sb", [128, 32], I32))
        ohT_sb = ee(nc.sbuf_tensor("ohT_sb", [64, 128], F32))

        for i in range(4):
            add("gpsimd", lambda g, i=i: g.dma_start(
                out=xbf_d[i * 512:(i + 1) * 512, :], in_=xs_d[i * 512:(i + 1) * 512, :]),
                (), f"dsx{i}")
        for i in range(4):
            add("gpsimd", lambda g, i=i: g.dma_start(
                out=wbf_k[:, i * 512:(i + 1) * 512], in_=wk_d[:, i * 512:(i + 1) * 512]),
                (), f"dsk{i}")
        cast_v, cast_q, cast_o = ("dsv", 16), ("dsq", 16), ("dso", 16)
        add("gpsimd", lambda g: g.dma_start(out=wbf_v, in_=wv_d), (("dsx3", 16),), "dsv")
        add("gpsimd", lambda g: g.dma_start(out=wbf_q, in_=wq_d), (("dsv", 16),), "dsq")
        add("gpsimd", lambda g: g.dma_start(out=wbf_o, in_=wo_d), (("dsq", 16),), "dso")

        with ExitStack() as ps:
            pe = ps.enter_context
            xT_sb = pe(nc.sbuf_tensor("xT_sb", [128, 16 * NLOC], BF16))
            wcol_sb = pe(nc.sbuf_tensor("wcol_sb", [128, 2 * 16 * 128], BF16))
            stage_sb = pe(nc.sbuf_tensor("stage_sb", [128, 2 * E], BF16))
            ps_a = pe(nc.psum_tensor("ps_a", [128, 512], F32))
            ps_b = pe(nc.psum_tensor("ps_b", [128, 512], F32))
            vps = [pe(nc.psum_tensor(f"vps{i}", [128, 128], F32)) for i in range(4)]

            xTv = xT_sb[:, :].rearrange("p (c n) -> p c n", c=16)
            wcv = wcol_sb[:, :].rearrange("p (b c d) -> p b c d", b=2, c=16)
            stv = stage_sb[:, :].rearrange("p (b n) -> p b n", b=2)

            xT_tb = {}
            for tb in range(4):
                for c in range(16):
                    for ti in range(4):
                        t = tb * 4 + ti
                        add("sync", lambda s, c=c, t=t: s.dma_start(
                            out=xTv[:, c, t * 128:(t + 1) * 128],
                            in_=xbf_d[t * 128:(t + 1) * 128, c * 128:(c + 1) * 128],
                            transpose=True), ((f"dsx{tb}", 16),), f"dxT{tb}")
                xT_tb[tb] = cnt[f"dxT{tb}"]

            def proj_phase(kind, wbf, cast_val, dst):
                SQ = {"k": "dqK", "v": "dqV", "q": "dqQ"}[kind]
                prev_mm = cnt["tn"]
                prev_stores = tuple((s, cnt[s]) for s in
                                    ("dqK0", "dqK1", "dqV0", "dqV1", "dqQ0", "dqQ1")
                                    if cnt[s] > 0)
                wl = {}
                mmv = {}
                evac_vc = {}
                store_dq = {}
                vsub_evac = {}
                for h in range(H):
                    buf = h % 2
                    wwaits = [(f"dsk{h // 4}", 16) if kind == "k" else cast_val]
                    if h >= 2 and (h - 2, 3) in mmv:
                        wwaits.append(("tn", mmv[(h - 2, 3)]))
                    else:
                        wwaits.append(("tn", prev_mm))
                    add("sync", lambda s, wbf=wbf, h=h, buf=buf: s.dma_start(
                        out=wcv[:, buf, :, :],
                        in_=wbf[:, h * 128:(h + 1) * 128].rearrange("(c p) d -> p c d", p=128)),
                        wwaits, f"dW{h % 2}")
                    wl[h] = cnt[f"dW{h % 2}"]
                    for tb in range(4):
                        gidx = h * 4 + tb
                        mmw = [(f"dW{h % 2}", wl[h]), (f"dxT{tb}", xT_tb[tb])]
                        if kind != "v":
                            psum = ps_a if gidx % 2 == 0 else ps_b
                            if gidx >= 2:
                                mmw.append(("vc", evac_vc[gidx - 2]))
                            for c in range(16):
                                add("tensor", lambda t, psum=psum, buf=buf, c=c, tb=tb:
                                    t.matmul(psum[:, :], wcv[:, buf, c, :],
                                             xTv[:, c, tb * 512:(tb + 1) * 512],
                                             start=(c == 0), stop=(c == 15)),
                                    mmw if c == 0 else (), "tn" if c == 15 else None)
                            mmv[(h, tb)] = cnt["tn"]
                            evac_vc[gidx] = vadd(
                                lambda v, psum=psum, buf=buf, tb=tb:
                                v.tensor_copy(stv[:, buf, tb * 512:(tb + 1) * 512], psum[:, :]),
                                (("tn", mmv[(h, tb)]),)
                                + (((f"{SQ}{h % 2}", store_dq[h - 2]),) if (h >= 2 and tb == 0 and (h - 2) in store_dq)
                                   else (prev_stores if (h < 2 and tb == 0) else ())))
                        else:
                            for sub in range(4):
                                w0 = list(mmw)
                                if gidx >= 1 and (gidx - 1, sub) in vsub_evac:
                                    w0.append(("vc", vsub_evac[(gidx - 1, sub)]))
                                for c in range(16):
                                    add("tensor", lambda t, sub=sub, buf=buf, c=c, tb=tb:
                                        t.matmul(vps[sub][:, :],
                                                 xTv[:, c, tb * 512 + sub * 128:tb * 512 + (sub + 1) * 128],
                                                 wcv[:, buf, c, :],
                                                 start=(c == 0), stop=(c == 15)),
                                        w0 if c == 0 else (), "tn" if c == 15 else None)
                                mmv[(h, tb, sub)] = cnt["tn"]
                            mmv[(h, tb)] = cnt["tn"]
                            for sub in range(4):
                                ww = [("tn", mmv[(h, tb, sub)])]
                                if h >= 2 and (h - 2, 3) in store_dq:
                                    ww.append((f"{SQ}{h % 2}", store_dq[(h - 2, 3)]))
                                elif h < 2:
                                    ww.extend(prev_stores)
                                vsub_evac[(gidx, sub)] = vadd(
                                    lambda v, sub=sub, buf=buf, tb=tb:
                                    v.tensor_copy(
                                        stv[:, buf, tb * 512 + sub * 128:tb * 512 + (sub + 1) * 128],
                                        vps[sub][:, :]),
                                    ww)
                            evac_vc[gidx] = cnt["vc"]
                            add("scalar", lambda s, dst=dst, h=h, tb=tb, buf=buf: s.dma_start(
                                out=dst[h * NLOC + tb * 512:h * NLOC + (tb + 1) * 512, :]
                                .rearrange("(s p) d -> p s d", p=128),
                                in_=stv[:, buf, tb * 512:(tb + 1) * 512].rearrange("p (s d) -> p s d", s=4)),
                                (("vc", evac_vc[gidx]),), f"{SQ}{h % 2}")
                            store_dq[(h, tb)] = cnt[f"{SQ}{h % 2}"]
                    if kind != "v":
                        add("scalar", lambda s, dst=dst, h=h, buf=buf: s.dma_start(
                            out=dst[h * 128:(h + 1) * 128, :], in_=stv[:, buf, :]),
                            (("vc", evac_vc[h * 4 + 3]),), f"{SQ}{h % 2}")
                        store_dq[h] = cnt[f"{SQ}{h % 2}"]
                    if kind in ("k", "v") and h % 4 == 3:
                        add("scalar", lambda s: s.nop(),
                            ((f"{SQ}0", cnt[f"{SQ}0"]), (f"{SQ}1", cnt[f"{SQ}1"]),
                             ("sck", cnt["sck"])), "sck")
                return (cnt[f"{SQ}0"], cnt[f"{SQ}1"])

            k_done = proj_phase("k", wbf_k, None, kT_in)
            for g4 in range(4):
                add("gpsimd", lambda g, g4=g4: g.collective_compute(
                    "AllGather", ALU.bypass, replica_groups=[[0, 1, 2, 3], [4, 5, 6, 7]],
                    ins=[kT_in[g4 * 512:(g4 + 1) * 512, :]],
                    outs=[kT_ag[g4 * 2048:(g4 + 1) * 2048, :]]),
                    (("sck", g4 + 1),), "cc")
                pass
            v_done = proj_phase("v", wbf_v, cast_v, v_in)
            for g4 in range(4):
                add("gpsimd", lambda g, g4=g4: g.collective_compute(
                    "AllGather", ALU.bypass, replica_groups=[[0, 1, 2, 3], [4, 5, 6, 7]],
                    ins=[v_in[g4 * 4 * NLOC:(g4 + 1) * 4 * NLOC, :]],
                    outs=[v_ag[g4 * 16 * NLOC:(g4 + 1) * 16 * NLOC, :]]),
                    (("sck", 4 + g4 + 1),), "cc")
                pass
            kv_cc = {"k": {g4: g4 + 1 for g4 in range(4)},
                     "v": {g4: 4 + g4 + 1 for g4 in range(4)}}
            ag_cc = cnt["cc"]

            q_done = proj_phase("q", wbf_q, cast_q, qT_scr)
            pP_tn = cnt["tn"]
            pP_vc = cnt["vc"]
            pP_stores = tuple((s, cnt[s]) for s in
                              ("dqK0", "dqK1", "dqV0", "dqV1", "dqQ0", "dqQ1"))

        for dst, src in ((alibik_sb, alibik_d), (tri_sb, tri_d), (iota64c_sb, iota64c_d),
                         (identf_sb, identf_d)):
            add("sync", lambda s, dst=dst, src=src: s.dma_start(out=dst[:, :], in_=src), (), "dC")
        for dst, src, w in ((tdiv32_sb, tdiv32_d, E), (gamma_sb, gamma_d, E), (beta_sb, beta_d, E),
                            (bo_sb, bo_d, E), (iota64_sb, iota64_d, 64), (tdiv8_sb, tdiv8_d, 64),
                            (iota8_sb, iota8_d, 8)):
            add("sync", lambda s, dst=dst, src=src, w=w:
                s.dma_start(out=dst[:, :], in_=src.broadcast_to([128, w])), (), "dC")
        add("sync", lambda s: s.dma_start(out=tabs_sb[:, 0:512], in_=bases_d), (), "dC")
        add("sync", lambda s: s.dma_start(out=tabs_sb[:, 512:1024], in_=slopes_d), (), "dC")
        add("sync", lambda s: s.dma_start(out=tabs_sb[:, 1024:1025], in_=oscale_d), (), "dC")
        add("sync", lambda s: s.dma_start(out=m64c[0:64, 0:512] if False else scra[0:64, 0:512],
                                          in_=slopes_d), (), "dC")
        consts_dh = cnt["dC"]

        s3v = scra[0:64, 0:512].rearrange("p (j g) -> p j g", g=16)
        w = 8
        first = True
        while w >= 1:
            vadd(lambda v, w=w: v.tensor_tensor(s3v[:, :, 0:w], s3v[:, :, 0:w],
                                                s3v[:, :, w:2 * w], ALU.add),
                 (("dC", consts_dh),) if first else ())
            first = False
            w //= 2
        vadd(lambda v: v.tensor_scalar_mul(m64a[0:64, 0:32], s3v[:, :, 0], 1.0 / 16.0))
        vadd(lambda v: v.tensor_single_scalar(m64b[0:64, 0:32], m64a[0:64, 0:32], 0.0, ALU.is_gt))
        vadd(lambda v: v.tensor_single_scalar(m64c[0:64, 0:32], m64a[0:64, 0:32], 0.0, ALU.is_lt))
        ssf_vc = vadd(lambda v: v.tensor_sub(m64b[0:64, 0:32], m64b[0:64, 0:32], m64c[0:64, 0:32]))
        add("sync", lambda s: s.dma_start(out=ssf_hbm, in_=m64b[0:64, 0:32]), (("vc", ssf_vc),), "dh")
        ssf_out_dh = cnt["dh"]
        add("sync", lambda s: s.dma_start(
            out=ssf_sb[:, :],
            in_=ssf_hbm.rearrange("a b -> (a b)").unsqueeze(0).broadcast_to([128, E])),
            (("dh", ssf_out_dh),), "dS2")
        ssf_dh = cnt["dS2"]

        vadd(lambda v: v.memset(accmin_sb[:, :], BIGI))
        accinit_vc = vadd(lambda v: v.memset(accmax_sb[:, :], -BIGI))

        wo_sb = ee(nc.sbuf_tensor("wo_sb", [128, 16 * E], BF16))
        qt_sb = ee(nc.sbuf_tensor("qt_sb", [128, 8 * 512], BF16))
        kbuf = ee(nc.sbuf_tensor("kbuf", [128, 68 * 128], BF16))
        vbuf = ee(nc.sbuf_tensor("vbuf", [128, 68 * 129], BF16))
        pT_sb = ee(nc.sbuf_tensor("pT_sb", [128, 2 * 1024], BF16))
        ctxst_sb = ee(nc.sbuf_tensor("ctxst_sb", [128, 2 * 512], BF16))
        ctxT_sb = ee(nc.sbuf_tensor("ctxT_sb", [128, 2 * 16 * 128], BF16))
        wov = wo_sb[:, :].rearrange("p (c e) -> p c e", c=16)
        vbv = vbuf[:, :].rearrange("p (t w) -> p t w", w=129)

        add("sync", lambda s: s.dma_start(
            out=wo_sb[:, :], in_=wbf_o.rearrange("(c p) e -> p c e", p=128)),
            (cast_o, ("tn", pP_tn), ("vc", pP_vc)) + pP_stores, "dWo")
        wo_dh = cnt["dWo"]
        ones_vc = vadd(lambda v: v.memset(vbv[:, :, 128:129], 1.0),
                       (("tn", pP_tn),) + pP_stores)

        psA = ExitStack()
        sT_ps = [psA.enter_context(nc.psum_tensor(f"sT{i}", [128, 1024], F32)) for i in range(2)]
        O_ps = [psA.enter_context(nc.psum_tensor(f"Ops{i}", [128, 129], F32))[:, :] for i in range(4)]
        wo_ps = [sT_ps[0][:, 0:512], sT_ps[0][:, 512:1024]]

        linv_sb = ee(nc.sbuf_tensor("linv_sb", [128, 4], F32))
        eps_sb = ee(nc.sbuf_tensor("eps_sb", [128, 1], F32))

        eps_vc = vadd(lambda v: v.memset(eps_sb[:, :], LN_EPS))

        reg_use = {r: pP_tn for r in list(range(4)) + ["diag"]}
        exp_hist = [0, 0]
        avp_hist = [0, 0]
        wo_evac_box = [0]
        tf_load_box = [0, 0]
        fin_sc_box = [0]
        qt_last_use_box = [pP_tn]
        h1_spill_dh = [0] * NTT
        ctx_store_dq_box = [0]
        tidx_exp_dh = [0] * NTT
        ctxT_use = [0, 0]
        x_free = [0]
        h1t_free = [0]
        woprev = [0, 0]
        ctx_dq_after_slot = [0] * 4
        ctxst_store = [None, None]

        def ln_stats_scalar(src_buf, waits):
            sadd(lambda s: s.activation(scra[:, :] if src_buf is not scra else scrb[:, :],
                                        src_buf[:, :], AF.Identity, accum_out=st8[:, 0:1]),
                 waits)
            sadd(lambda s: s.activation(scra[:, :] if src_buf is not scra else scrb[:, :],
                                        src_buf[:, :], AF.Square, accum_out=st8[:, 1:2]))
            sadd(lambda s: s.mul(st8[:, 2:3], st8[:, 0:1], 1.0 / E))
            sadd(lambda s: s.mul(st8[:, 3:4], st8[:, 1:2], 1.0 / E))
            sadd(lambda s: s.activation(st8[:, 4:5], st8[:, 2:3], AF.Square))
            sadd(lambda s: s.mul(st8[:, 4:5], st8[:, 4:5], -1.0))
            sadd(lambda s: s.add(st8[:, 3:4], st8[:, 3:4], st8[:, 4:5]))
            sadd(lambda s: s.activation(st8[:, 5:6], st8[:, 3:4], AF.Ln, bias=eps_sb[:, 0:1]))
            return sadd(lambda s: s.activation(st8[:, 6:7], st8[:, 5:6], AF.Exp, scale=-0.5))

        def kv_loads(p, h):
            tfb = h % 2
            add("sync", lambda s, p=p, h=h, tfb=tfb: s.dma_start(
                out=tf_sb[:, tfb * 68:(tfb + 1) * 68],
                in_=tilefac_d[p * H + h:p * H + h + 1, :].broadcast_to([128, 68])),
                (("vc", tf_load_box[tfb]),), f"dTf{tfb}")
            tf_dh = (f"dTf{tfb}", cnt[f"dTf{tfb}"])
            scale_vc = {}
            res = {}
            for r in range(4):
                ns = sum(1 for pp in range(4) if QB[r][pp] <= 4 * p + 2)
                if ns == 0:
                    continue
                sk = f"dK{r}"
                g4, hm = h // 4, h % 4
                add("sync", lambda s, r=r, g4=g4, hm=hm, ns=ns: s.dma_start(
                    out=kbuf[:, r * 2048:r * 2048 + ns * 512],
                    in_=kT_ag[g4 * 2048 + r * 512 + hm * 128:g4 * 2048 + r * 512 + (hm + 1) * 128,
                              0:ns * 512]),
                    (("cc", kv_cc["k"][g4]), ("tn", reg_use.get(r, 0))), sk)
                add("sync", lambda s, r=r, g4=g4, hm=hm, ns=ns: s.dma_start(
                    out=vbv[:, r * 16:r * 16 + ns * 4, 0:128],
                    in_=v_ag[g4 * 16 * NLOC + r * 4 * NLOC + hm * NLOC:
                             g4 * 16 * NLOC + r * 4 * NLOC + hm * NLOC + ns * 512, :]
                    .rearrange("(t p) d -> p t d", p=128)),
                    (("cc", kv_cc["v"][g4]), ("vc", ones_vc)), sk)
                res[r] = (sk, cnt[sk])
                scale_vc[r] = scale_region(r, r * 16, ns * 4, res[r], tf_dh, tfb)
            add("sync", lambda s, p=p, h=h: s.dma_start(
                out=kbuf[:, 64 * 128:64 * 128 + 512],
                in_=kT_in[h * 128:(h + 1) * 128, p * 512:(p + 1) * 512]),
                (("dqK0", k_done[0]), ("dqK1", k_done[1]),
                 ("tn", reg_use.get("diag", 0))), "dKd")
            add("sync", lambda s, p=p, h=h: s.dma_start(
                out=vbv[:, 64:68, 0:128],
                in_=v_in[h * NLOC + p * 512:h * NLOC + (p + 1) * 512, :]
                .rearrange("(t p) d -> p t d", p=128)),
                (("dqV0", v_done[0]), ("dqV1", v_done[1]), ("vc", ones_vc)), "dKd")
            res["diag"] = ("dKd", cnt["dKd"])
            scale_vc["diag"] = scale_region("diag", 64, 4, res["diag"], tf_dh, tfb)
            tf_load_box[tfb] = max(scale_vc.values())
            return res, scale_vc

        def scale_region(reg, t0, nt, kvv, tf_dh, tfb):
            w1 = [(kvv[0], kvv[1]), tf_dh, ("tn", reg_use.get(reg, 0))]
            vadd(lambda v, t0=t0, nt=nt, tfb=tfb: v.tensor_tensor(
                vbv[:, t0:t0 + nt, 0:128], vbv[:, t0:t0 + nt, 0:128],
                tf_sb[:, tfb * 68 + t0:tfb * 68 + t0 + nt].unsqueeze(2)
                .broadcast_to([128, nt, 128]), ALU.mult), w1)
            return vadd(lambda v, t0=t0, nt=nt, tfb=tfb: v.tensor_copy(
                vbv[:, t0:t0 + nt, 128], tf_sb[:, tfb * 68 + t0:tfb * 68 + t0 + nt]))

        def flash(p, h, qt_dh):
            tl = tiles_for(p)
            L = len(tl)
            kv, scale_vc = kv_loads(p, h)

            def emit_av(i):
                reg, sl = tl[i]
                diag = reg == "diag"
                dk = sl - 64 if diag else 0
                pbase = ((i // 2) % 2) * 1024 + (i % 2) * 512
                avw = [("sc", exp_done[i // 2]), (kv[reg][0], kv[reg][1]),
                       ("vc", scale_vc[reg])]
                if diag:
                    mval = vadd(lambda v, pbase=pbase, dk=dk: v.tensor_tensor(
                        pT_sb[:, pbase + dk * 128:pbase + (dk + 1) * 128],
                        pT_sb[:, pbase + dk * 128:pbase + (dk + 1) * 128],
                        tri_sb[:, :], ALU.mult), (("sc", exp_done[i // 2]),))
                    avw.append(("vc", mval))
                subs = list(range(dk, 4)) if diag else list(range(4))
                for sub in subs:
                    add("tensor", lambda t, pbase=pbase, sl=sl, sub=sub, i=i, L=L:
                        t.matmul(O_ps[sub],
                                 pT_sb[:, pbase + sub * 128:pbase + (sub + 1) * 128],
                                 vbv[:, sl, :],
                                 start=(i == 0), stop=(i == L - 4 + sub)),
                        avw if sub == subs[0] else (), "tn" if sub == subs[-1] else None)
                reg_use[reg] = cnt["tn"]

            exp_done = {}
            for i, (reg, sl) in enumerate(tl):
                par = (i // 2) % 2
                sw = [(kv[reg][0], kv[reg][1]), ("dQt", qt_dh), ("sc", exp_hist[par]),
                      ("vc", wo_evac_box[0])]
                if i <= 3:
                    sw.append(("sc", fin_sc_box[0]))
                add("tensor", lambda t, par=par, i=i, sl=sl, hh=h % 8:
                    t.matmul(sT_ps[par][:, (i % 2) * 512:(i % 2) * 512 + 512],
                             kbuf[:, sl * 128:(sl + 1) * 128],
                             qt_sb[:, hh * 512:(hh + 1) * 512],
                             start=True, stop=True), sw, "tn" if i % 2 == 1 else None)
                if i % 2 == 1:
                    m = i // 2
                    ew = [("tn", cnt["tn"]), ("tn", avp_hist[m % 2])]
                    exp_done[m] = sadd(lambda s, par=par, h=h, m=m:
                                       s.activation(pT_sb[:, (m % 2) * 1024:(m % 2) * 1024 + 1024],
                                                    sT_ps[par][:, :], AF.Exp,
                                                    bias=alibik_sb[:, h:h + 1], scale=INV), ew)
                    exp_hist[par] = exp_done[m]
                    if m >= 1:
                        emit_av(2 * (m - 1))
                        emit_av(2 * (m - 1) + 1)
                        avp_hist[(m - 1) % 2] = cnt["tn"]
            emit_av(L - 2)
            emit_av(L - 1)
            avp_hist[(L // 2 - 1) % 2] = cnt["tn"]
            avlast = cnt["tn"]
            rvals = []
            for sub in range(4):
                rvals.append(vadd(lambda v, sub=sub: v.reciprocal(linv_sb[:, sub:sub + 1],
                                                                  O_ps[sub][:, 128:129]),
                                  (("tn", avlast),)))
            buf = h % 2
            csem = f"dqC{p}{buf}"
            for sub in range(4):
                mw = [("vc", rvals[sub])]
                if sub == 0 and ctxst_store[buf] is not None:
                    mw.append(ctxst_store[buf])
                sadd(lambda s, sub=sub, buf=buf: s.mul(
                    ctxst_sb[:, buf * 512 + sub * 128:buf * 512 + (sub + 1) * 128],
                    O_ps[sub][:, 0:128], linv_sb[:, sub:sub + 1]), mw)
            fin_sc_box[0] = cnt["sc"]
            add("scalar", lambda s, p=p, h=h, buf=buf: s.dma_start(
                out=ctx_hbm[p * 512:(p + 1) * 512, h * 128:(h + 1) * 128]
                .rearrange("(s p) d -> p s d", p=128),
                in_=ctxst_sb[:, buf * 512:(buf + 1) * 512].rearrange("p (s d) -> p s d", s=4)),
                (("sc", fin_sc_box[0]),), csem)
            ctxst_store[buf] = (csem, cnt[csem])

        wff_pref = {}

        def emit_wff_prefetch(p, tt):
            cb = tt % 2
            for c in range(16):
                add("sync", lambda s, tt=tt, c=c, cb=cb: s.dma_start(
                    out=ctxT_sb[:, cb * 2048 + c * 128:cb * 2048 + (c + 1) * 128],
                    in_=ctx_hbm[tt * 128:(tt + 1) * 128, c * 128:(c + 1) * 128],
                    transpose=True),
                    ((ctx_dq_after_slot[p][0], ctx_dq_after_slot[p][1],
                      ("tn", ctxT_use[cb])) if c == 0 else ()), f"dCT{cb}")
            ctxT_dh = cnt[f"dCT{cb}"]
            xw = [("vc", h1t_free[0])]
            if tt - 1 >= 0:
                xw.append(("dSp", h1_spill_dh[tt - 1]))
            add("gpsimd", lambda s, tt=tt: s.dma_start(
                out=h1t_sb[:, :], in_=xs_d[tt * 128:(tt + 1) * 128, :]), xw, "dX")
            wff_pref[tt] = (ctxT_dh, cnt["dX"])

        def emit_wff_tile(p, tt):
            cb = tt % 2
            ctxT_dh, xld_dh = wff_pref[tt]
            vadd(lambda v: v.tensor_add(h1t_sb[:, :], h1t_sb[:, :], bo_sb[:, :]),
                 (("dX", xld_dh),))
            for e4 in range(4):
                ww = [(f"dCT{cb}", ctxT_dh), ("dWo", wo_dh), ("vc", woprev[e4 % 2]),
                      ("sc", max(exp_hist))]
                for c in range(16):
                    add("tensor", lambda t, e4=e4, c=c, cb=cb: t.matmul(
                        wo_ps[e4 % 2][:, :], ctxT_sb[:, cb * 2048 + c * 128:cb * 2048 + (c + 1) * 128],
                        wov[:, c, e4 * 512:(e4 + 1) * 512],
                        start=(c == 0), stop=(c == 15)),
                        ww if c == 0 else (), "tn" if c == 15 else None)
                wo_mm = cnt["tn"]
                sl = slice(e4 * 512, (e4 + 1) * 512)
                vadd(lambda v, e4=e4, sl=sl: v.scalar_tensor_tensor(
                    h1t_sb[:, sl], wo_ps[e4 % 2][:, :], 1.0, h1t_sb[:, sl],
                    ALU.mult, ALU.add), (("tn", wo_mm),))
                woprev[e4 % 2] = cnt["vc"]
            ctxT_use[cb] = cnt["tn"]
            hpre_vc = cnt["vc"]
            wo_evac_box[0] = cnt["vc"]
            rstd_sc = ln_stats_scalar(h1t_sb, (("vc", hpre_vc),))
            h1_vc = vadd(lambda v: v.tensor_scalar(h1t_sb[:, :], h1t_sb[:, :], st8[:, 2:3],
                                                   st8[:, 6:7], ALU.subtract, ALU.mult),
                         (("sc", rstd_sc),))
            add("gpsimd", lambda s, tt=tt: s.dma_start(
                out=h1_spill[tt * 128:(tt + 1) * 128, :], in_=h1t_sb[:, :]),
                (("vc", h1_vc),), "dSp")
            h1_spill_dh[tt] = cnt["dSp"]
            emit_ffront(tt, h1_vc)

        def emit_ffront(tt, h1_vc):
            rstd_sc = ln_stats_scalar(h1t_sb, (("vc", h1_vc),))
            vadd(lambda v: v.tensor_scalar(x_sb[:, :], h1t_sb[:, :], st8[:, 2:3], st8[:, 6:7],
                                           ALU.subtract, ALU.mult), (("sc", rstd_sc),))
            h1t_free[0] = cnt["vc"]
            vadd(lambda v: v.tensor_mul(x_sb[:, :], x_sb[:, :], gamma_sb[:, :]))
            xn_vc = vadd(lambda v: v.tensor_add(x_sb[:, :], x_sb[:, :], beta_sb[:, :]),
                         (("dS2", ssf_dh),))
            sg = sadd(lambda s: s.activation(scra[:, :], x_sb[:, :], AF.Sign), (("vc", xn_vc),))
            vadd(lambda v: v.tensor_mul(scra[:, :], scra[:, :], ssf_sb[:, :]), (("sc", sg),))
            a3 = scra[:, :].rearrange("p (t j) -> p t j", t=64)
            w = 16
            while w >= 1:
                vadd(lambda v, w=w: v.tensor_tensor(a3[:, :, 0:w], a3[:, :, 0:w],
                                                    a3[:, :, w:2 * w], ALU.add))
                w //= 2
            vadd(lambda v: v.tensor_copy(ts64_sb[:, :], a3[:, :, 0]))
            vadd(lambda v: v.tensor_copy(m64c[:, :], ts64_sb[:, :]))
            c4 = m64c[:, :].rearrange("p (c u) -> p c u", c=8)
            w = 4
            while w >= 1:
                vadd(lambda v, w=w: v.tensor_tensor(c4[:, :, 0:w], c4[:, :, 0:w],
                                                    c4[:, :, w:2 * w], ALU.add))
                w //= 2
            vadd(lambda v: v.tensor_copy(st8b[:, 0:8], c4[:, :, 0]))
            vadd(lambda v: v.reduce_max(st8[:, 0:1], st8b[:, 0:8], AX.X))
            vadd(lambda v: v.tensor_single_scalar(st8b[:, 0:8], st8b[:, 0:8], st8[:, 0:1],
                                                  ALU.is_equal))
            vadd(lambda v: v.tensor_scalar(m64a[:, 0:8], st8b[:, 0:8], BIGI, BIGI,
                                           ALU.mult, ALU.subtract))
            vadd(lambda v: v.tensor_mul(st8b[:, 0:8], st8b[:, 0:8], iota8_sb[:, :]))
            vadd(lambda v: v.tensor_sub(st8b[:, 0:8], st8b[:, 0:8], m64a[:, 0:8]))
            vadd(lambda v: v.tensor_reduce(st8[:, 1:2], st8b[:, 0:8], AX.X, ALU.min))
            vadd(lambda v: v.tensor_single_scalar(m64a[:, :], tdiv8_sb[:, :], st8[:, 1:2],
                                                  ALU.is_equal))
            vadd(lambda v: v.tensor_mul(m64b[:, :], ts64_sb[:, :], m64a[:, :]))
            vadd(lambda v: v.tensor_scalar(m64c[:, :], m64a[:, :], BIGI, BIGI,
                                           ALU.mult, ALU.subtract))
            vadd(lambda v: v.tensor_add(m64b[:, :], m64b[:, :], m64c[:, :]))
            vadd(lambda v: v.reduce_max(st8[:, 2:3], m64b[:, :], AX.X))
            vadd(lambda v: v.tensor_single_scalar(m64a[:, :], m64b[:, :], st8[:, 2:3],
                                                  ALU.is_equal))
            vadd(lambda v: v.tensor_scalar(m64c[:, :], m64a[:, :], BIGI, BIGI,
                                           ALU.mult, ALU.subtract))
            vadd(lambda v: v.tensor_mul(m64a[:, :], m64a[:, :], iota64_sb[:, :]))
            vadd(lambda v: v.tensor_sub(m64a[:, :], m64a[:, :], m64c[:, :]))
            tidx_vc = vadd(lambda v, tt=tt: v.tensor_reduce(tidx_sb[:, tt:tt + 1], m64a[:, :],
                                                            AX.X, ALU.min))
            vadd(lambda v, tt=tt: v.tensor_single_scalar(scrb[:, :], tdiv32_sb[:, :],
                                                         tidx_sb[:, tt:tt + 1], ALU.is_equal))
            vadd(lambda v: v.tensor_mul(scrb[:, :], scrb[:, :], x_sb[:, :]))
            w = 1024
            while w >= 32:
                vadd(lambda v, w=w: v.tensor_tensor(scrb[:, 0:w], scrb[:, 0:w],
                                                    scrb[:, w:2 * w], ALU.add))
                w //= 2
            vadd(lambda v, tt=tt: v.tensor_copy(xsl_sb[:, tt * 32:(tt + 1) * 32], scrb[:, 0:32]))
            vadd(lambda v, tt=tt: v.tensor_reduce(st8[:, 3:4], xsl_sb[:, tt * 32:(tt + 1) * 32],
                                                  AX.X, ALU.min))
            vadd(lambda v, tt=tt: v.reduce_max(st8[:, 4:5], xsl_sb[:, tt * 32:(tt + 1) * 32], AX.X))
            vadd(lambda v, tt=tt: v.tensor_single_scalar(m64a[:, :], iota64_sb[:, :],
                                                         tidx_sb[:, tt:tt + 1], ALU.is_equal))
            vadd(lambda v: v.tensor_single_scalar(m64b[:, :], m64a[:, :], st8[:, 3:4], ALU.mult))
            vadd(lambda v: v.tensor_scalar(m64c[:, :], m64a[:, :], BIGI, BIGI,
                                           ALU.mult, ALU.subtract))
            vadd(lambda v: v.tensor_sub(m64b[:, :], m64b[:, :], m64c[:, :]))
            vadd(lambda v: v.tensor_tensor(accmin_sb[:, :], accmin_sb[:, :], m64b[:, :], ALU.min),
                 (("vc", accinit_vc),))
            vadd(lambda v: v.tensor_single_scalar(m64b[:, :], m64a[:, :], st8[:, 4:5], ALU.mult))
            vadd(lambda v: v.tensor_add(m64b[:, :], m64b[:, :], m64c[:, :]))
            vadd(lambda v: v.tensor_tensor(accmax_sb[:, :], accmax_sb[:, :], m64b[:, :], ALU.max))
            add("gpsimd", lambda s, tt=tt: s.dma_start(
                out=tidx_hbm[tt:tt + 1, :].rearrange("a p -> p a"),
                in_=tidx_sb[:, tt:tt + 1]), (("vc", tidx_vc),), "dTx")
            tidx_exp_dh[tt] = cnt["dTx"]

        pending = None
        for p in range(4):
            for h in range(H):
                if h % 8 == 0:
                    half = h // 8
                    add("sync", lambda s, p=p, half=half: s.dma_start(
                        out=qt_sb[:, :].rearrange("p (h q) -> p h q", h=8),
                        in_=qT_scr[half * 1024:(half + 1) * 1024, p * 512:(p + 1) * 512]
                        .rearrange("(h d) q -> d h q", d=128)),
                        (("dqQ0", q_done[0]), ("dqQ1", q_done[1]),
                         ("tn", qt_last_use_box[0])), "dQt")
                    qt_dh = cnt["dQt"]
                if pending is not None and h % 4 == 0:
                    emit_wff_prefetch(pending, 4 * pending + h // 4)
                flash(p, h, qt_dh)
                qt_last_use_box[0] = cnt["tn"]
                if pending is not None and h % 4 == 3:
                    emit_wff_tile(pending, 4 * pending + h // 4)
            ctx_dq_after_slot[p] = ((f"dqC{p}0", cnt[f"dqC{p}0"]), (f"dqC{p}1", cnt[f"dqC{p}1"]))
            pending = p
        for tt in range(12, 16):
            emit_wff_prefetch(3, tt)
            emit_wff_tile(3, tt)

        tr1 = add("tensor", lambda t: t.transpose(sT_ps[0][0:64, 0:128], accmin_sb[:, :],
                                                  identf_sb[:, :]),
                  (("vc", cnt["vc"]), ("sc", fin_sc_box[0])), "tn")
        tr2 = add("tensor", lambda t: t.transpose(sT_ps[1][0:64, 0:128], accmax_sb[:, :],
                                                  identf_sb[:, :]), (), "tn")
        vadd(lambda v: v.tensor_reduce(mm2_sb[:, 0:1], sT_ps[0][0:64, 0:128], AX.X, ALU.min),
             (("tn", tr1),))
        vadd(lambda v: v.reduce_max(mm2_sb[:, 1:2], sT_ps[1][0:64, 0:128], AX.X), (("tn", tr2),))
        pack_vc = vadd(lambda v: v.tensor_scalar_mul(mm2_sb[:, 1:2], mm2_sb[:, 1:2], -1.0))
        add("gpsimd", lambda g: g.dma_start(out=ar_in_d, in_=mm2_sb[:, :]),
            (("vc", pack_vc),), "ds")
        arin_ds = cnt["ds"]
        add("gpsimd", lambda g: g.collective_compute(
            "AllReduce", ALU.min, replica_groups=[[0, 1, 2, 3, 4, 5, 6, 7]],
            ins=[ar_in_d], outs=[ar_out_d]), (("ds", arin_ds),), "cc")
        ar_cc = cnt["cc"]
        add("gpsimd", lambda g: g.dma_start(out=mm2_sb[:, :], in_=ar_out_d),
            (("cc", ar_cc),), "ds")
        arback_ds = cnt["ds"]
        vadd(lambda v: v.tensor_copy(tabs_sb[:, 1025:1026], mm2_sb[:, 0:1]), (("ds", arback_ds),))
        tabmm_vc = vadd(lambda v: v.tensor_scalar_mul(tabs_sb[:, 1026:1027], mm2_sb[:, 1:2], -1.0))

        psA.close()
        psB = ExitStack()
        tab_ps = psB.enter_context(nc.psum_tensor("tab_ps", [128, 1027], F32))

        ohT_use = 0
        tab_evac = 0
        tbufs = [scra, x_sb]
        tab_mm_done = [0] * (NTT + 1)
        store_done = [0] * (NTT + 1)
        h2_done = [0] * (NTT + 1)
        stats3_done = [0] * (NTT + 1)

        def fback_A(tt):
            nonlocal ohT_use, tab_evac
            tb = tbufs[tt % 2]
            add("sync", lambda s, tt=tt: s.dma_start(
                out=ohT_sb[:, :],
                in_=tidx_hbm[tt:tt + 1, :].broadcast_to([64, 128])),
                (("dTx", cnt["dTx"]), ("tn", tab_mm_done[tt - 1] if tt else 0),
                 ("vc", ohT_use)), "dOH")
            ohload_dh = cnt["dOH"]
            ohT_vc = vadd(lambda v: v.tensor_single_scalar(ohT_sb[:, :], ohT_sb[:, :],
                                                           iota64c_sb[:, :], ALU.is_equal),
                          (("dOH", ohload_dh),))
            ohT_use = ohT_vc
            tmw = [("vc", ohT_vc), ("vc", tabmm_vc), ("vc", tab_evac), ("sc", cnt["sc"]),
                   ("dh", store_done[tt - 2] if tt >= 2 else 0)]
            add("tensor", lambda t: t.matmul(tab_ps[:, 0:512], ohT_sb[:, :], tabs_sb[:, 0:512],
                                             start=True, stop=True), tmw, None)
            add("tensor", lambda t: t.matmul(tab_ps[:, 512:1024], ohT_sb[:, :],
                                             tabs_sb[:, 512:1024], start=True, stop=True), (), None)
            tab_mm_done[tt] = add("tensor", lambda t: t.matmul(
                tab_ps[:, 1024:1027], ohT_sb[:, :], tabs_sb[:, 1024:1027],
                start=True, stop=True), (), "tn")
            tab_evac = vadd(lambda v, tb=tb: v.tensor_copy(tb[:, 0:1027], tab_ps[:, :]),
                            (("tn", tab_mm_done[tt]),
                             ("dh", store_done[tt - 2] if tt >= 2 else 0)))

        fback_A(0)
        for tt in range(NTT):
            tb = tbufs[tt % 2]
            add("sync", lambda s, tt=tt: s.dma_start(
                out=h1t_sb[:, :], in_=h1_spill[tt * 128:(tt + 1) * 128, :]),
                (("dSp", cnt["dSp"]), ("vc", h2_done[tt - 1] if tt else h1t_free[0])), "dHR")
            h1re_dh = cnt["dHR"]
            vadd(lambda v, tb=tb: v.tensor_sub(st8[:, 0:1], tb[:, 1026:1027], tb[:, 1025:1026]),
                 (("vc", tab_evac),))
            vadd(lambda v: v.tensor_scalar_add(st8[:, 0:1], st8[:, 0:1], 1e-8))
            vadd(lambda v: v.reciprocal(st8[:, 1:2], st8[:, 0:1]))
            vadd(lambda v, tt=tt, tb=tb: v.tensor_scalar(
                scrb[:, 0:32], xsl_sb[:, tt * 32:(tt + 1) * 32],
                tb[:, 1025:1026], st8[:, 1:2], ALU.subtract, ALU.mult),
                (("sc", stats3_done[tt - 1] if tt else 0),))
            vadd(lambda v: v.tensor_scalar(scrb[:, 0:32], scrb[:, 0:32], 0.0, 16.0,
                                           ALU.max, ALU.mult))
            vadd(lambda v: v.tensor_scalar_min(scrb[:, 0:32], scrb[:, 0:32], UCLIP16))
            vadd(lambda v: v.tensor_copy(i32_sb[:, :], scrb[:, 0:32]))
            vadd(lambda v: v.tensor_copy(scrb[:, 32:64], i32_sb[:, :]))
            vadd(lambda v: v.tensor_tensor(scrb[:, 64:96], scrb[:, 32:64], scrb[:, 0:32],
                                           ALU.is_gt))
            vadd(lambda v: v.tensor_sub(scrb[:, 32:64], scrb[:, 32:64], scrb[:, 64:96]))
            vadd(lambda v: v.tensor_sub(scrb[:, 64:96], scrb[:, 0:32], scrb[:, 32:64]))
            tv3 = tb[:, 0:1024].rearrange("p (jj g) -> p jj g", g=16)
            tv4 = tb[:, 0:1024].rearrange("p (a j g) -> p a j g", a=2, j=32)
            gi_ap = scrb[:, 32:64]
            vadd(lambda v: v.tensor_single_scalar(m64a[:, 0:32], gi_ap, 8.0, ALU.is_ge))
            vadd(lambda v: v.scalar_tensor_tensor(m64a[:, 32:64], m64a[:, 0:32], -8.0, gi_ap,
                                                  ALU.mult, ALU.add))
            for k in (3, 2, 1, 0):
                w = 1 << k
                if k < 3:
                    vadd(lambda v, w=w: v.tensor_single_scalar(m64a[:, 0:32], m64a[:, 32:64],
                                                               float(w), ALU.is_ge))
                    if k > 0:
                        vadd(lambda v, w=w: v.scalar_tensor_tensor(
                            m64a[:, 32:64], m64a[:, 0:32], -float(w), m64a[:, 32:64],
                            ALU.mult, ALU.add))
                vadd(lambda v, w=w, tv3=tv3: v.tensor_tensor(
                    tv3[:, :, w:2 * w], tv3[:, :, w:2 * w], tv3[:, :, 0:w], ALU.subtract))
                for a in range(2):
                    vadd(lambda v, w=w, a=a, tv4=tv4: v.tensor_tensor(
                        tv4[:, a, :, w:2 * w], tv4[:, a, :, w:2 * w],
                        m64a[:, 0:32].unsqueeze(2).broadcast_to([128, 32, w]), ALU.mult))
                vadd(lambda v, w=w, tv3=tv3: v.tensor_tensor(
                    tv3[:, :, 0:w], tv3[:, :, 0:w], tv3[:, :, w:2 * w], ALU.add))
            vadd(lambda v, tv3=tv3: v.tensor_mul(m64c[:, 0:32], tv3[:, 32:64, 0],
                                                 scrb[:, 64:96]))
            vadd(lambda v, tv3=tv3: v.tensor_add(m64c[:, 0:32], m64c[:, 0:32],
                                                 tv3[:, 0:32, 0]))
            vadd(lambda v, tb=tb: v.tensor_single_scalar(m64c[:, 0:32], m64c[:, 0:32],
                                                         tb[:, 1024:1025], ALU.mult))
            if tt + 1 < NTT:
                fback_A(tt + 1)
            vadd(lambda v, tt=tt: v.tensor_single_scalar(scrb[:, :], tdiv32_sb[:, :],
                                                         tidx_sb[:, tt:tt + 1], ALU.is_equal))
            vadd(lambda v: v.tensor_tensor(
                scrb[:, :].rearrange("p (t j) -> p t j", t=64),
                scrb[:, :].rearrange("p (t j) -> p t j", t=64),
                m64c[:, 0:32].rearrange("p (a j) -> p a j", a=1).broadcast_to([128, 64, 32]),
                ALU.mult))
            h2_done[tt] = vadd(lambda v: v.scalar_tensor_tensor(
                scrb[:, :], h1t_sb[:, :], 2.0, scrb[:, :], ALU.mult, ALU.add),
                (("dHR", h1re_dh),))
            sadd(lambda s, tb=tb: s.activation(tb[:, :], scrb[:, :], AF.Identity,
                                               accum_out=st8[:, 0:1]), (("vc", h2_done[tt]),))
            sadd(lambda s, tb=tb: s.activation(tb[:, :], scrb[:, :], AF.Square,
                                               accum_out=st8[:, 1:2]))
            sadd(lambda s: s.mul(st8[:, 2:3], st8[:, 0:1], 1.0 / E))
            sadd(lambda s: s.mul(st8[:, 3:4], st8[:, 1:2], 1.0 / E))
            sadd(lambda s: s.activation(st8[:, 4:5], st8[:, 2:3], AF.Square))
            sadd(lambda s: s.mul(st8[:, 4:5], st8[:, 4:5], -1.0))
            stats3_done[tt] = sadd(lambda s: s.add(st8[:, 3:4], st8[:, 3:4], st8[:, 4:5]))
            sadd(lambda s: s.activation(st8[:, 5:6], st8[:, 3:4], AF.Ln, bias=eps_sb[:, 0:1]))
            rstd_sc = sadd(lambda s: s.activation(st8[:, 6:7], st8[:, 5:6], AF.Exp, scale=-0.5))
            outv = vadd(lambda v, tb=tb: v.tensor_scalar(tb[:, :], scrb[:, :], st8[:, 2:3],
                                                         st8[:, 6:7], ALU.subtract, ALU.mult),
                        (("sc", rstd_sc),))
            add("sync", lambda s, tt=tt, tb=tb: s.dma_start(
                out=out_d[tt * 128:(tt + 1) * 128, :], in_=tb[:, :]), (("vc", outv),), "dh")
            store_done[tt] = cnt["dh"]

        final_waits = [(k, cnt[k]) for k in semkeys if cnt[k] > 0 and k != "gc"]
        add("sync", lambda s: s.nop(), final_waits, None)

        for ek in ENGS:
            def make(ek):
                def body(eng):
                    last = {}
                    for waits, emitfn, inc, n in prog[ek]:
                        for skey, val in waits:
                            if val > 0 and last.get(skey, 0) < val:
                                eng.wait_ge(sem[skey], val)
                                last[skey] = val
                        inst = emitfn(eng)
                        if inc is not None:
                            inst.then_inc(sem[inc], n)
                return body
            getattr(block, ek)(make(ek))

        psB.close()
    return nc


def _host_consts():
    slopes = alibi = None
    sl = 2.0 ** (-8.0 * np.arange(1, H + 1) / H)
    tri = (np.arange(128)[:, None] <= np.arange(128)[None, :]).astype(ml_dtypes.bfloat16)
    tdiv32 = (np.arange(E) // 32).astype(ml_dtypes.bfloat16)[None, :]
    iota64 = np.arange(64, dtype=np.float32)[None, :]
    tdiv8 = (np.arange(64) // 8).astype(np.float32)[None, :]
    iota8 = np.arange(8, dtype=np.float32)[None, :]
    iota64c = np.arange(64, dtype=np.float32)[:, None]
    identf = np.eye(128, dtype=np.float32)
    return sl, tri, tdiv32, iota64, tdiv8, iota8, iota64c, identf


def _alibik():
    sl = 2.0 ** (-8.0 * np.arange(1, H + 1) / H)
    kk = np.arange(128, dtype=np.float64)
    return (kk[:, None] * sl[None, :] * INV).astype(np.float32)


def _tilefac_table(r):
    sl = 2.0 ** (-8.0 * np.arange(1, H + 1) / H)
    tab = np.zeros((4 * H, 68), dtype=np.float32)
    for p in range(4):
        own_j = QB[r][p]
        q0 = 512 * own_j
        for h in range(H):
            row = p * H + h
            for slot in range(64):
                r2, pj, sub = slot // 16, (slot % 16) // 4, slot % 4
                j = QB[r2][pj]
                if j < own_j:
                    k0 = 512 * j + 128 * sub
                    tab[row, slot] = np.float32(np.exp(sl[h] * (k0 - q0) * INV))
            for dk in range(4):
                tab[row, 64 + dk] = np.float32(np.exp(sl[h] * (128 * dk) * INV))
    return tab


_NC_CACHE = {}


def make_in_maps(L0_attn_x, L0_attn_Wq, L0_attn_Wk, L0_attn_Wv, L0_attn_Wo, L0_attn_bo,
                 L0_ffn_ln_gamma, L0_ffn_ln_beta, L0_ffn_spline_bases, L0_ffn_spline_slopes,
                 L0_ffn_output_scale):
    x = np.ascontiguousarray(np.asarray(L0_attn_x, dtype=np.float32))
    wq = np.ascontiguousarray(np.asarray(L0_attn_Wq, dtype=np.float32))
    wk = np.ascontiguousarray(np.asarray(L0_attn_Wk, dtype=np.float32))
    wv = np.ascontiguousarray(np.asarray(L0_attn_Wv, dtype=np.float32))
    wo = np.ascontiguousarray(np.asarray(L0_attn_Wo, dtype=np.float32))
    bo = np.asarray(L0_attn_bo, dtype=np.float32)[None, :]
    gamma = np.asarray(L0_ffn_ln_gamma, dtype=np.float32)[None, :]
    beta = np.asarray(L0_ffn_ln_beta, dtype=np.float32)[None, :]
    bases = np.ascontiguousarray(np.asarray(L0_ffn_spline_bases, dtype=np.float32).reshape(64, 512))
    slopes = np.ascontiguousarray(np.asarray(L0_ffn_spline_slopes, dtype=np.float32).reshape(64, 512))
    oscale = np.asarray(L0_ffn_output_scale, dtype=np.float32)[:, None]

    slv, tri, tdiv32, iota64, tdiv8, iota8, iota64c, identf = _host_consts()

    in_maps = []
    for core in range(8):
        b = core // 4
        r = core % 4
        xs = np.ascontiguousarray(
            np.concatenate([x[b, 512 * j:512 * (j + 1), :] for j in QB[r]], axis=0))
        in_maps.append({
            "xs": xs, "wq": wq, "wk": wk, "wv": wv, "wo": wo, "bo": bo,
            "gamma": gamma, "beta": beta, "bases": bases, "slopes": slopes,
            "oscale": oscale, "alibik": _alibik(), "tilefac": _tilefac_table(r), "tri": tri,
            "tdiv32": tdiv32, "iota64": iota64, "tdiv8": tdiv8, "iota8": iota8,
            "iota64c": iota64c, "identf": identf,
        })

    return in_maps


def assemble_out(core_outs):
    out = np.empty((2, S, E), dtype=np.float32)
    for core in range(8):
        b = core // 4
        r = core % 4
        o = core_outs[core]
        for p, j in enumerate(QB[r]):
            out[b, 512 * j:512 * (j + 1), :] = o[512 * p:512 * (p + 1), :]
    return out


def kernel(**inputs):
    if "nc" not in _NC_CACHE:
        _NC_CACHE["nc"] = build_nc()
    nc = _NC_CACHE["nc"]
    in_maps = make_in_maps(**inputs)
    res = run_bass_kernel_spmd(nc, in_maps, core_ids=list(range(8)))
    return assemble_out([res.results[c]["out"] for c in range(8)])


# version 12
# speedup vs baseline: 1.0526x; 1.0526x over previous
import numpy as np
import ml_dtypes
from contextlib import ExitStack

import concourse.bass as bass
import concourse.mybir as mybir
from concourse.bass_utils import run_bass_kernel_spmd

F32 = mybir.dt.float32
BF16 = mybir.dt.bfloat16
I32 = mybir.dt.int32
AF = mybir.ActivationFunctionType
ALU = mybir.AluOpType
AX = mybir.AxisListType

S = 8192
E = 2048
H = 16
DH = 128
NLOC = 2048
NTT = 16
INV = float(np.float32(1.0) / np.float32(np.sqrt(np.float32(128.0))))
LN_EPS = 1e-5
NEG = -1.0e30
BIGI = 1.0e9
UCLIP16 = float(np.float32(16.0 * np.float32(1.0 - 1e-6)))

QB = [[r, 7 - r, 8 + r, 15 - r] for r in range(4)]
SLOT_BASE = [0, 16, 48, 96]
TOT_TILES = 160

ENGS = ("sync", "gpsimd", "tensor", "vector", "scalar")


def owner_of(j):
    p = j // 4
    q = j % 4
    r = q if p % 2 == 0 else 3 - q
    return r, p


def tiles_for(p):
    lst = []
    for j in range(4 * p + 3):
        r, pj = owner_of(j)
        for sub in range(4):
            lst.append((r, r * 16 + pj * 4 + sub))
    for sub in range(4):
        lst.append(("diag", 64 + sub))
    return lst


def build_nc(debug=False):
    nc = bass.Bass("TRN2", target_bir_lowering=False)

    xs_d = nc.dram_tensor("xs", [NLOC, E], F32, kind="ExternalInput").ap()
    wq_d = nc.dram_tensor("wq", [E, E], F32, kind="ExternalInput").ap()
    wk_d = nc.dram_tensor("wk", [E, E], F32, kind="ExternalInput").ap()
    wv_d = nc.dram_tensor("wv", [E, E], F32, kind="ExternalInput").ap()
    wo_d = nc.dram_tensor("wo", [E, E], F32, kind="ExternalInput").ap()
    bo_d = nc.dram_tensor("bo", [1, E], F32, kind="ExternalInput").ap()
    gamma_d = nc.dram_tensor("gamma", [1, E], F32, kind="ExternalInput").ap()
    beta_d = nc.dram_tensor("beta", [1, E], F32, kind="ExternalInput").ap()
    bases_d = nc.dram_tensor("bases", [64, 512], F32, kind="ExternalInput").ap()
    slopes_d = nc.dram_tensor("slopes", [64, 512], F32, kind="ExternalInput").ap()
    oscale_d = nc.dram_tensor("oscale", [64, 1], F32, kind="ExternalInput").ap()
    alibik_d = nc.dram_tensor("alibik", [128, H], F32, kind="ExternalInput").ap()
    tilefac_d = nc.dram_tensor("tilefac", [4 * H, 68], F32, kind="ExternalInput").ap()
    tri_d = nc.dram_tensor("tri", [128, 128], BF16, kind="ExternalInput").ap()
    identb_d = nc.dram_tensor("identb", [128, 128], BF16, kind="ExternalInput").ap()
    tdiv32_d = nc.dram_tensor("tdiv32", [1, E], BF16, kind="ExternalInput").ap()
    iota64_d = nc.dram_tensor("iota64", [1, 64], F32, kind="ExternalInput").ap()
    tdiv8_d = nc.dram_tensor("tdiv8", [1, 64], F32, kind="ExternalInput").ap()
    iota8_d = nc.dram_tensor("iota8", [1, 8], F32, kind="ExternalInput").ap()
    iota64c_d = nc.dram_tensor("iota64c", [64, 1], F32, kind="ExternalInput").ap()
    identf_d = nc.dram_tensor("identf", [128, 128], F32, kind="ExternalInput").ap()

    out_d = nc.dram_tensor("out", [NLOC, E], F32, kind="ExternalOutput").ap()

    def scrt(name, shape, dtype):
        kind = "ExternalOutput" if debug else "Internal"
        return nc.dram_tensor(name, shape, dtype, kind=kind).ap()

    wbf_q = nc.dram_tensor("wbf_q", [E, E], BF16, kind="Internal").ap()
    wbf_k = nc.dram_tensor("wbf_k", [E, E], BF16, kind="Internal").ap()
    wbf_v = nc.dram_tensor("wbf_v", [E, E], BF16, kind="Internal").ap()
    wbf_o = nc.dram_tensor("wbf_o", [E, E], BF16, kind="Internal").ap()
    qT_scr = scrt("qT_scr", [E, NLOC], BF16)
    kT_in = scrt("kT_in", [E, NLOC], BF16)
    v_in = scrt("v_in", [H * NLOC, DH], BF16)
    kT_ag = nc.dram_tensor("kT_ag", [4 * E, NLOC], BF16, kind="Internal").ap()
    v_ag = nc.dram_tensor("v_ag", [4 * H * NLOC, DH], BF16, kind="Internal").ap()
    ctx_hbm = scrt("ctx_hbm", [NLOC, E], BF16)
    h1_spill = scrt("h1_spill", [NLOC, E], F32)
    ssf_hbm = nc.dram_tensor("ssf_hbm", [64, 32], F32, kind="Internal").ap()
    tidx_hbm = nc.dram_tensor("tidx_hbm", [NTT, 128], F32, kind="Internal").ap()
    ar_in_d = nc.dram_tensor("ar_in", [64, 2], F32, kind="Internal").ap()
    ar_out_d = nc.dram_tensor("ar_out", [64, 2], F32, kind="Internal").ap()

    semkeys = (["dh", "ds", "cc", "tn", "vc", "sc", "gc",
                "dsx0", "dsx1", "dsx2", "dsx3", "dsk0", "dsk1", "dsk2", "dsk3",
                "dsv", "dsq", "dso", "sck",
                "dC", "dS2", "dWo", "dxT", "dW0", "dW1", "dQt",
                "dK0", "dK1", "dK2", "dK3", "dKd", "dX", "dCT0", "dCT1", "dHR", "dOH", "dSp", "dTx",
                "dxT0", "dxT1", "dxT2", "dxT3", "dTf0", "dTf1"]
               + [f"dqK{i}" for i in range(2)] + [f"dqV{i}" for i in range(2)]
               + [f"dqQ{i}" for i in range(2)]
               + [f"dqC{p}{i}" for p in range(4) for i in range(2)])
    cnt = {k: 0 for k in semkeys}
    prog = {k: [] for k in ENGS}

    def add(ek, emit, waits=(), inc=None, n=None):
        if n is None:
            n = 16 if (inc is not None and inc.startswith("d")) else 1
        prog[ek].append((tuple(waits), emit, inc, n))
        if inc is not None:
            cnt[inc] += n
            return cnt[inc]
        return None

    def vadd(emit, waits=()):
        return add("vector", emit, tuple(waits) + (("vc", cnt["vc"]),), "vc")

    def sadd(emit, waits=()):
        return add("scalar", emit, tuple(waits) + (("sc", cnt["sc"]),), "sc")

    def gadd(emit, waits=()):
        return add("gpsimd", emit, tuple(waits) + (("gc", cnt["gc"]),), "gc")

    with ExitStack() as stack:
        ee = stack.enter_context
        sem = {k: ee(nc.semaphore(k)) for k in cnt}
        block = ee(nc.Block())

        alibik_sb = ee(nc.sbuf_tensor("alibik_sb", [128, H], F32))
        tf_sb = ee(nc.sbuf_tensor("tf_sb", [128, 2 * 68], F32))
        tri_sb = ee(nc.sbuf_tensor("tri_sb", [128, 128], BF16))
        tdiv32_sb = ee(nc.sbuf_tensor("tdiv32_sb", [128, E], BF16))
        gamma_sb = ee(nc.sbuf_tensor("gamma_sb", [128, E], F32))
        beta_sb = ee(nc.sbuf_tensor("beta_sb", [128, E], F32))
        bo_sb = ee(nc.sbuf_tensor("bo_sb", [128, E], F32))
        ssf_sb = ee(nc.sbuf_tensor("ssf_sb", [128, E], F32))
        iota64_sb = ee(nc.sbuf_tensor("iota64_sb", [128, 64], F32))
        tdiv8_sb = ee(nc.sbuf_tensor("tdiv8_sb", [128, 64], F32))
        iota8_sb = ee(nc.sbuf_tensor("iota8_sb", [128, 8], F32))
        iota64c_sb = ee(nc.sbuf_tensor("iota64c_sb", [64, 1], F32))
        identf_sb = ee(nc.sbuf_tensor("identf_sb", [128, 128], F32))
        identb_sb = ee(nc.sbuf_tensor("identb_sb", [128, 128], BF16))
        tabs_sb = ee(nc.sbuf_tensor("tabs_sb", [64, 1027], F32))
        xsl_sb = ee(nc.sbuf_tensor("xsl_sb", [128, NTT * 32], F32))
        tidx_sb = ee(nc.sbuf_tensor("tidx_sb", [128, NTT], F32))
        accmin_sb = ee(nc.sbuf_tensor("accmin_sb", [128, 64], F32))
        accmax_sb = ee(nc.sbuf_tensor("accmax_sb", [128, 64], F32))
        mm2_sb = ee(nc.sbuf_tensor("mm2_sb", [64, 2], F32))
        x_sb = ee(nc.sbuf_tensor("x_sb", [128, E], F32))
        h1t_sb = ee(nc.sbuf_tensor("h1t_sb", [128, E], F32))
        scra = ee(nc.sbuf_tensor("scra", [128, E], F32))
        scrb = ee(nc.sbuf_tensor("scrb", [128, E], F32))
        st8 = ee(nc.sbuf_tensor("st8", [128, 8], F32))
        st8b = ee(nc.sbuf_tensor("st8b", [128, 8], F32))
        m64a = ee(nc.sbuf_tensor("m64a", [128, 64], F32))
        m64b = ee(nc.sbuf_tensor("m64b", [128, 64], F32))
        m64c = ee(nc.sbuf_tensor("m64c", [128, 64], F32))
        ts64_sb = ee(nc.sbuf_tensor("ts64_sb", [128, 64], F32))
        i32_sb = ee(nc.sbuf_tensor("i32_sb", [128, 32], I32))
        ohT_sb = ee(nc.sbuf_tensor("ohT_sb", [64, 128], F32))

        add("sync", lambda s: s.dma_start(out=identb_sb[:, :], in_=identb_d), (), "dC")
        add("sync", lambda s: s.dma_start(out=identf_sb[:, :], in_=identf_d), (), "dC")
        for i in range(4):
            add("gpsimd", lambda g, i=i: g.dma_start(
                out=wbf_k[:, i * 512:(i + 1) * 512], in_=wk_d[:, i * 512:(i + 1) * 512]),
                (), f"dsk{i}")
        cast_v, cast_q, cast_o = ("dsv", 16), ("dsq", 16), ("dso", 16)
        add("gpsimd", lambda g: g.dma_start(out=wbf_v, in_=wv_d), (("dsk3", 16),), "dsv")
        add("gpsimd", lambda g: g.dma_start(out=wbf_q, in_=wq_d), (("dsv", 16),), "dsq")
        add("gpsimd", lambda g: g.dma_start(out=wbf_o, in_=wo_d), (("dsq", 16),), "dso")

        with ExitStack() as ps:
            pe = ps.enter_context
            xT_sb = pe(nc.sbuf_tensor("xT_sb", [128, 16 * NLOC], BF16))
            wcol_sb = pe(nc.sbuf_tensor("wcol_sb", [128, 2 * 16 * 128], BF16))
            stage_sb = pe(nc.sbuf_tensor("stage_sb", [128, 2 * E], BF16))
            ps_a = pe(nc.psum_tensor("ps_a", [128, 512], F32))
            ps_b = pe(nc.psum_tensor("ps_b", [128, 512], F32))
            vps = [pe(nc.psum_tensor(f"vps{i}", [128, 128], F32)) for i in range(4)]

            xTv = xT_sb[:, :].rearrange("p (c n) -> p c n", c=16)
            wcv = wcol_sb[:, :].rearrange("p (b c d) -> p b c d", b=2, c=16)
            stv = stage_sb[:, :].rearrange("p (b n) -> p b n", b=2)

            xbstg = pe(nc.sbuf_tensor("xbstg", [128, 4 * E], F32))
            tn0 = cnt["tn"]
            evac_after_tile = {}
            xpose_evac = {}
            xg = 0
            for t in range(16):
                lq = "sync" if t % 2 == 0 else "scalar"
                lsem = f"dxT{t % 4}"
                lw = []
                if t - 4 >= 0:
                    lw.append(("vc", evac_after_tile[t - 4]))
                add(lq, lambda s, t=t: s.dma_start(
                    out=xbstg[:, (t % 4) * E:(t % 4) * E + E],
                    in_=xs_d[t * 128:(t + 1) * 128, :]), lw, lsem)
                ld = (lsem, cnt[lsem])
                for e4 in range(4):
                    psum = ps_a if xg % 2 == 0 else ps_b
                    tw = [ld, ("dC", 32)]
                    if xg - 2 >= 0:
                        tw.append(("vc", xpose_evac[xg - 2]))
                    for j in range(4):
                        add("tensor", lambda tt, t=t, e4=e4, j=j, psum=psum: tt.transpose(
                            psum[:, j * 128:(j + 1) * 128],
                            xbstg[:, (t % 4) * E + (4 * e4 + j) * 128:
                                  (t % 4) * E + (4 * e4 + j + 1) * 128],
                            identf_sb[:, :]),
                            tw if j == 0 else (), "tn" if j == 3 else None)
                    xpose_evac[xg] = vadd(
                        lambda v, t=t, e4=e4, psum=psum: v.tensor_copy(
                            xTv[:, 4 * e4:4 * e4 + 4, t * 128:(t + 1) * 128],
                            psum[:, :].rearrange("p (c w) -> p c w", c=4)),
                        (("tn", cnt["tn"]),))
                    xg += 1
                evac_after_tile[t] = cnt["vc"]
            xT_tb = {tb: evac_after_tile[4 * tb + 3] for tb in range(4)}
            xpose_last = cnt["vc"]

            def proj_phase(kind, wbf, cast_val, dst, prev_mm0=None):
                SQ = {"k": "dqK", "v": "dqV", "q": "dqQ"}[kind]
                prev_mm = cnt["tn"] if prev_mm0 is None else prev_mm0
                prev_stores = tuple((s, cnt[s]) for s in
                                    ("dqK0", "dqK1", "dqV0", "dqV1", "dqQ0", "dqQ1")
                                    if cnt[s] > 0)
                wl = {}
                mmv = {}
                evac_vc = {}
                store_dq = {}
                vsub_evac = {}
                for h in range(H):
                    buf = h % 2
                    wwaits = [(f"dsk{h // 4}", 16) if kind == "k" else cast_val]
                    if h >= 2 and (h - 2, 3) in mmv:
                        wwaits.append(("tn", mmv[(h - 2, 3)]))
                    else:
                        wwaits.append(("tn", prev_mm))
                    add("sync", lambda s, wbf=wbf, h=h, buf=buf: s.dma_start(
                        out=wcv[:, buf, :, :],
                        in_=wbf[:, h * 128:(h + 1) * 128].rearrange("(c p) d -> p c d", p=128)),
                        wwaits, f"dW{h % 2}")
                    wl[h] = cnt[f"dW{h % 2}"]
                    for tb in range(4):
                        gidx = h * 4 + tb
                        mmw = [(f"dW{h % 2}", wl[h]), ("vc", xT_tb[tb])]
                        if kind != "v":
                            psum = ps_a if gidx % 2 == 0 else ps_b
                            if gidx >= 2:
                                mmw.append(("vc", evac_vc[gidx - 2]))
                            elif kind == "k":
                                mmw.append(("vc", xpose_last))
                            for c in range(16):
                                add("tensor", lambda t, psum=psum, buf=buf, c=c, tb=tb:
                                    t.matmul(psum[:, :], wcv[:, buf, c, :],
                                             xTv[:, c, tb * 512:(tb + 1) * 512],
                                             start=(c == 0), stop=(c == 15)),
                                    mmw if c == 0 else (), "tn" if c == 15 else None)
                            mmv[(h, tb)] = cnt["tn"]
                            evac_vc[gidx] = vadd(
                                lambda v, psum=psum, buf=buf, tb=tb:
                                v.tensor_copy(stv[:, buf, tb * 512:(tb + 1) * 512], psum[:, :]),
                                (("tn", mmv[(h, tb)]),)
                                + (((f"{SQ}{h % 2}", store_dq[h - 2]),) if (h >= 2 and tb == 0 and (h - 2) in store_dq)
                                   else (prev_stores if (h < 2 and tb == 0) else ())))
                        else:
                            for sub in range(4):
                                w0 = list(mmw)
                                if gidx >= 1 and (gidx - 1, sub) in vsub_evac:
                                    w0.append(("vc", vsub_evac[(gidx - 1, sub)]))
                                for c in range(16):
                                    add("tensor", lambda t, sub=sub, buf=buf, c=c, tb=tb:
                                        t.matmul(vps[sub][:, :],
                                                 xTv[:, c, tb * 512 + sub * 128:tb * 512 + (sub + 1) * 128],
                                                 wcv[:, buf, c, :],
                                                 start=(c == 0), stop=(c == 15)),
                                        w0 if c == 0 else (), "tn" if c == 15 else None)
                                mmv[(h, tb, sub)] = cnt["tn"]
                            mmv[(h, tb)] = cnt["tn"]
                            for sub in range(4):
                                ww = [("tn", mmv[(h, tb, sub)])]
                                if h >= 2 and (h - 2, 3) in store_dq:
                                    ww.append((f"{SQ}{h % 2}", store_dq[(h - 2, 3)]))
                                elif h < 2:
                                    ww.extend(prev_stores)
                                vsub_evac[(gidx, sub)] = vadd(
                                    lambda v, sub=sub, buf=buf, tb=tb:
                                    v.tensor_copy(
                                        stv[:, buf, tb * 512 + sub * 128:tb * 512 + (sub + 1) * 128],
                                        vps[sub][:, :]),
                                    ww)
                            evac_vc[gidx] = cnt["vc"]
                            add("scalar", lambda s, dst=dst, h=h, tb=tb, buf=buf: s.dma_start(
                                out=dst[h * NLOC + tb * 512:h * NLOC + (tb + 1) * 512, :]
                                .rearrange("(s p) d -> p s d", p=128),
                                in_=stv[:, buf, tb * 512:(tb + 1) * 512].rearrange("p (s d) -> p s d", s=4)),
                                (("vc", evac_vc[gidx]),), f"{SQ}{h % 2}")
                            store_dq[(h, tb)] = cnt[f"{SQ}{h % 2}"]
                    if kind != "v":
                        add("scalar", lambda s, dst=dst, h=h, buf=buf: s.dma_start(
                            out=dst[h * 128:(h + 1) * 128, :], in_=stv[:, buf, :]),
                            (("vc", evac_vc[h * 4 + 3]),), f"{SQ}{h % 2}")
                        store_dq[h] = cnt[f"{SQ}{h % 2}"]
                    if kind in ("k", "v") and h % 4 == 3:
                        add("scalar", lambda s: s.nop(),
                            ((f"{SQ}0", cnt[f"{SQ}0"]), (f"{SQ}1", cnt[f"{SQ}1"]),
                             ("sck", cnt["sck"])), "sck")
                return (cnt[f"{SQ}0"], cnt[f"{SQ}1"])

            k_done = proj_phase("k", wbf_k, None, kT_in, prev_mm0=tn0)
            for g4 in range(4):
                add("gpsimd", lambda g, g4=g4: g.collective_compute(
                    "AllGather", ALU.bypass, replica_groups=[[0, 1, 2, 3], [4, 5, 6, 7]],
                    ins=[kT_in[g4 * 512:(g4 + 1) * 512, :]],
                    outs=[kT_ag[g4 * 2048:(g4 + 1) * 2048, :]]),
                    (("sck", g4 + 1),), "cc")
                pass
            v_done = proj_phase("v", wbf_v, cast_v, v_in)
            for g4 in range(4):
                add("gpsimd", lambda g, g4=g4: g.collective_compute(
                    "AllGather", ALU.bypass, replica_groups=[[0, 1, 2, 3], [4, 5, 6, 7]],
                    ins=[v_in[g4 * 4 * NLOC:(g4 + 1) * 4 * NLOC, :]],
                    outs=[v_ag[g4 * 16 * NLOC:(g4 + 1) * 16 * NLOC, :]]),
                    (("sck", 4 + g4 + 1),), "cc")
                pass
            kv_cc = {"k": {g4: g4 + 1 for g4 in range(4)},
                     "v": {g4: 4 + g4 + 1 for g4 in range(4)}}
            ag_cc = cnt["cc"]

            q_done = proj_phase("q", wbf_q, cast_q, qT_scr)
            pP_tn = cnt["tn"]
            pP_vc = cnt["vc"]
            pP_stores = tuple((s, cnt[s]) for s in
                              ("dqK0", "dqK1", "dqV0", "dqV1", "dqQ0", "dqQ1"))

        for dst, src in ((alibik_sb, alibik_d), (tri_sb, tri_d), (iota64c_sb, iota64c_d)):
            add("sync", lambda s, dst=dst, src=src: s.dma_start(out=dst[:, :], in_=src), (), "dC")
        for dst, src, w in ((tdiv32_sb, tdiv32_d, E), (gamma_sb, gamma_d, E), (beta_sb, beta_d, E),
                            (bo_sb, bo_d, E), (iota64_sb, iota64_d, 64), (tdiv8_sb, tdiv8_d, 64),
                            (iota8_sb, iota8_d, 8)):
            add("sync", lambda s, dst=dst, src=src, w=w:
                s.dma_start(out=dst[:, :], in_=src.broadcast_to([128, w])), (), "dC")
        add("sync", lambda s: s.dma_start(out=tabs_sb[:, 0:512], in_=bases_d), (), "dC")
        add("sync", lambda s: s.dma_start(out=tabs_sb[:, 512:1024], in_=slopes_d), (), "dC")
        add("sync", lambda s: s.dma_start(out=tabs_sb[:, 1024:1025], in_=oscale_d), (), "dC")
        add("sync", lambda s: s.dma_start(out=m64c[0:64, 0:512] if False else scra[0:64, 0:512],
                                          in_=slopes_d), (), "dC")
        consts_dh = cnt["dC"]

        s3v = scra[0:64, 0:512].rearrange("p (j g) -> p j g", g=16)
        w = 8
        first = True
        while w >= 1:
            vadd(lambda v, w=w: v.tensor_tensor(s3v[:, :, 0:w], s3v[:, :, 0:w],
                                                s3v[:, :, w:2 * w], ALU.add),
                 (("dC", consts_dh),) if first else ())
            first = False
            w //= 2
        vadd(lambda v: v.tensor_scalar_mul(m64a[0:64, 0:32], s3v[:, :, 0], 1.0 / 16.0))
        vadd(lambda v: v.tensor_single_scalar(m64b[0:64, 0:32], m64a[0:64, 0:32], 0.0, ALU.is_gt))
        vadd(lambda v: v.tensor_single_scalar(m64c[0:64, 0:32], m64a[0:64, 0:32], 0.0, ALU.is_lt))
        ssf_vc = vadd(lambda v: v.tensor_sub(m64b[0:64, 0:32], m64b[0:64, 0:32], m64c[0:64, 0:32]))
        add("sync", lambda s: s.dma_start(out=ssf_hbm, in_=m64b[0:64, 0:32]), (("vc", ssf_vc),), "dh")
        ssf_out_dh = cnt["dh"]
        add("sync", lambda s: s.dma_start(
            out=ssf_sb[:, :],
            in_=ssf_hbm.rearrange("a b -> (a b)").unsqueeze(0).broadcast_to([128, E])),
            (("dh", ssf_out_dh),), "dS2")
        ssf_dh = cnt["dS2"]

        vadd(lambda v: v.memset(accmin_sb[:, :], BIGI))
        accinit_vc = vadd(lambda v: v.memset(accmax_sb[:, :], -BIGI))

        wo_sb = ee(nc.sbuf_tensor("wo_sb", [128, 16 * E], BF16))
        qt_sb = ee(nc.sbuf_tensor("qt_sb", [128, 8 * 512], BF16))
        kbuf = ee(nc.sbuf_tensor("kbuf", [128, 68 * 128], BF16))
        vbuf = ee(nc.sbuf_tensor("vbuf", [128, 68 * 129], BF16))
        pT_sb = ee(nc.sbuf_tensor("pT_sb", [128, 2 * 1024], BF16))
        ctxst_sb = ee(nc.sbuf_tensor("ctxst_sb", [128, 2 * 512], BF16))
        ctxT_sb = ee(nc.sbuf_tensor("ctxT_sb", [128, 2 * 16 * 128], BF16))
        wov = wo_sb[:, :].rearrange("p (c e) -> p c e", c=16)
        vbv = vbuf[:, :].rearrange("p (t w) -> p t w", w=129)

        add("sync", lambda s: s.dma_start(
            out=wo_sb[:, :], in_=wbf_o.rearrange("(c p) e -> p c e", p=128)),
            (cast_o, ("tn", pP_tn), ("vc", pP_vc)) + pP_stores, "dWo")
        wo_dh = cnt["dWo"]
        ones_vc = vadd(lambda v: v.memset(vbv[:, :, 128:129], 1.0),
                       (("tn", pP_tn),) + pP_stores)

        psA = ExitStack()
        sT_ps = [psA.enter_context(nc.psum_tensor(f"sT{i}", [128, 1024], F32)) for i in range(2)]
        O_ps = [psA.enter_context(nc.psum_tensor(f"Ops{i}", [128, 129], F32))[:, :] for i in range(4)]
        wo_ps = [sT_ps[0][:, 0:512], sT_ps[0][:, 512:1024]]

        linv_sb = ee(nc.sbuf_tensor("linv_sb", [128, 4], F32))
        eps_sb = ee(nc.sbuf_tensor("eps_sb", [128, 1], F32))

        eps_vc = vadd(lambda v: v.memset(eps_sb[:, :], LN_EPS))

        reg_use = {r: pP_tn for r in list(range(4)) + ["diag"]}
        exp_hist = [0, 0]
        avp_hist = [0, 0]
        wo_evac_box = [0]
        tf_load_box = [0, 0]
        fin_sc_box = [0]
        qt_last_use_box = [pP_tn]
        h1_spill_dh = [0] * NTT
        ctx_store_dq_box = [0]
        tidx_exp_dh = [0] * NTT
        ctxT_use = [0, 0]
        x_free = [0]
        h1t_free = [0]
        woprev = [0, 0]
        ctx_dq_after_slot = [0] * 4
        ctxst_store = [None, None]

        def ln_stats_scalar(src_buf, waits):
            sadd(lambda s: s.activation(scra[:, :] if src_buf is not scra else scrb[:, :],
                                        src_buf[:, :], AF.Identity, accum_out=st8[:, 0:1]),
                 waits)
            sadd(lambda s: s.activation(scra[:, :] if src_buf is not scra else scrb[:, :],
                                        src_buf[:, :], AF.Square, accum_out=st8[:, 1:2]))
            sadd(lambda s: s.mul(st8[:, 2:3], st8[:, 0:1], 1.0 / E))
            sadd(lambda s: s.mul(st8[:, 3:4], st8[:, 1:2], 1.0 / E))
            sadd(lambda s: s.activation(st8[:, 4:5], st8[:, 2:3], AF.Square))
            sadd(lambda s: s.mul(st8[:, 4:5], st8[:, 4:5], -1.0))
            sadd(lambda s: s.add(st8[:, 3:4], st8[:, 3:4], st8[:, 4:5]))
            sadd(lambda s: s.activation(st8[:, 5:6], st8[:, 3:4], AF.Ln, bias=eps_sb[:, 0:1]))
            return sadd(lambda s: s.activation(st8[:, 6:7], st8[:, 5:6], AF.Exp, scale=-0.5))

        def kv_loads(p, h):
            tfb = h % 2
            add("sync", lambda s, p=p, h=h, tfb=tfb: s.dma_start(
                out=tf_sb[:, tfb * 68:(tfb + 1) * 68],
                in_=tilefac_d[p * H + h:p * H + h + 1, :].broadcast_to([128, 68])),
                (("vc", tf_load_box[tfb]),), f"dTf{tfb}")
            tf_dh = (f"dTf{tfb}", cnt[f"dTf{tfb}"])
            scale_vc = {}
            res = {}
            for r in range(4):
                ns = sum(1 for pp in range(4) if QB[r][pp] <= 4 * p + 2)
                if ns == 0:
                    continue
                sk = f"dK{r}"
                g4, hm = h // 4, h % 4
                add("sync", lambda s, r=r, g4=g4, hm=hm, ns=ns: s.dma_start(
                    out=kbuf[:, r * 2048:r * 2048 + ns * 512],
                    in_=kT_ag[g4 * 2048 + r * 512 + hm * 128:g4 * 2048 + r * 512 + (hm + 1) * 128,
                              0:ns * 512]),
                    (("cc", kv_cc["k"][g4]), ("tn", reg_use.get(r, 0))), sk)
                add("sync", lambda s, r=r, g4=g4, hm=hm, ns=ns: s.dma_start(
                    out=vbv[:, r * 16:r * 16 + ns * 4, 0:128],
                    in_=v_ag[g4 * 16 * NLOC + r * 4 * NLOC + hm * NLOC:
                             g4 * 16 * NLOC + r * 4 * NLOC + hm * NLOC + ns * 512, :]
                    .rearrange("(t p) d -> p t d", p=128)),
                    (("cc", kv_cc["v"][g4]), ("vc", ones_vc)), sk)
                res[r] = (sk, cnt[sk])
                scale_vc[r] = scale_region(r, r * 16, ns * 4, res[r], tf_dh, tfb)
            add("sync", lambda s, p=p, h=h: s.dma_start(
                out=kbuf[:, 64 * 128:64 * 128 + 512],
                in_=kT_in[h * 128:(h + 1) * 128, p * 512:(p + 1) * 512]),
                (("dqK0", k_done[0]), ("dqK1", k_done[1]),
                 ("tn", reg_use.get("diag", 0))), "dKd")
            add("sync", lambda s, p=p, h=h: s.dma_start(
                out=vbv[:, 64:68, 0:128],
                in_=v_in[h * NLOC + p * 512:h * NLOC + (p + 1) * 512, :]
                .rearrange("(t p) d -> p t d", p=128)),
                (("dqV0", v_done[0]), ("dqV1", v_done[1]), ("vc", ones_vc)), "dKd")
            res["diag"] = ("dKd", cnt["dKd"])
            scale_vc["diag"] = scale_region("diag", 64, 4, res["diag"], tf_dh, tfb)
            tf_load_box[tfb] = max(scale_vc.values())
            return res, scale_vc

        def scale_region(reg, t0, nt, kvv, tf_dh, tfb):
            w1 = [(kvv[0], kvv[1]), tf_dh, ("tn", reg_use.get(reg, 0))]
            vadd(lambda v, t0=t0, nt=nt, tfb=tfb: v.tensor_tensor(
                vbv[:, t0:t0 + nt, 0:128], vbv[:, t0:t0 + nt, 0:128],
                tf_sb[:, tfb * 68 + t0:tfb * 68 + t0 + nt].unsqueeze(2)
                .broadcast_to([128, nt, 128]), ALU.mult), w1)
            return vadd(lambda v, t0=t0, nt=nt, tfb=tfb: v.tensor_copy(
                vbv[:, t0:t0 + nt, 128], tf_sb[:, tfb * 68 + t0:tfb * 68 + t0 + nt]))

        def flash(p, h, qt_dh):
            tl = tiles_for(p)
            L = len(tl)
            kv, scale_vc = kv_loads(p, h)

            def emit_av(i):
                reg, sl = tl[i]
                diag = reg == "diag"
                dk = sl - 64 if diag else 0
                pbase = ((i // 2) % 2) * 1024 + (i % 2) * 512
                avw = [("sc", exp_done[i // 2]), (kv[reg][0], kv[reg][1]),
                       ("vc", scale_vc[reg])]
                if diag:
                    mval = vadd(lambda v, pbase=pbase, dk=dk: v.tensor_tensor(
                        pT_sb[:, pbase + dk * 128:pbase + (dk + 1) * 128],
                        pT_sb[:, pbase + dk * 128:pbase + (dk + 1) * 128],
                        tri_sb[:, :], ALU.mult), (("sc", exp_done[i // 2]),))
                    avw.append(("vc", mval))
                subs = list(range(dk, 4)) if diag else list(range(4))
                for sub in subs:
                    add("tensor", lambda t, pbase=pbase, sl=sl, sub=sub, i=i, L=L:
                        t.matmul(O_ps[sub],
                                 pT_sb[:, pbase + sub * 128:pbase + (sub + 1) * 128],
                                 vbv[:, sl, :],
                                 start=(i == 0), stop=(i == L - 4 + sub)),
                        avw if sub == subs[0] else (), "tn" if sub == subs[-1] else None)
                reg_use[reg] = cnt["tn"]

            exp_done = {}
            for i, (reg, sl) in enumerate(tl):
                par = (i // 2) % 2
                sw = [(kv[reg][0], kv[reg][1]), ("dQt", qt_dh), ("sc", exp_hist[par]),
                      ("vc", wo_evac_box[0])]
                if i <= 3:
                    sw.append(("sc", fin_sc_box[0]))
                add("tensor", lambda t, par=par, i=i, sl=sl, hh=h % 8:
                    t.matmul(sT_ps[par][:, (i % 2) * 512:(i % 2) * 512 + 512],
                             kbuf[:, sl * 128:(sl + 1) * 128],
                             qt_sb[:, hh * 512:(hh + 1) * 512],
                             start=True, stop=True), sw, "tn" if i % 2 == 1 else None)
                if i % 2 == 1:
                    m = i // 2
                    ew = [("tn", cnt["tn"]), ("tn", avp_hist[m % 2])]
                    exp_done[m] = sadd(lambda s, par=par, h=h, m=m:
                                       s.activation(pT_sb[:, (m % 2) * 1024:(m % 2) * 1024 + 1024],
                                                    sT_ps[par][:, :], AF.Exp,
                                                    bias=alibik_sb[:, h:h + 1], scale=INV), ew)
                    exp_hist[par] = exp_done[m]
                    if m >= 1:
                        emit_av(2 * (m - 1))
                        emit_av(2 * (m - 1) + 1)
                        avp_hist[(m - 1) % 2] = cnt["tn"]
            emit_av(L - 2)
            emit_av(L - 1)
            avp_hist[(L // 2 - 1) % 2] = cnt["tn"]
            avlast = cnt["tn"]
            rvals = []
            for sub in range(4):
                rvals.append(vadd(lambda v, sub=sub: v.reciprocal(linv_sb[:, sub:sub + 1],
                                                                  O_ps[sub][:, 128:129]),
                                  (("tn", avlast),)))
            buf = h % 2
            csem = f"dqC{p}{buf}"
            for sub in range(4):
                mw = [("vc", rvals[sub])]
                if sub == 0 and ctxst_store[buf] is not None:
                    mw.append(ctxst_store[buf])
                sadd(lambda s, sub=sub, buf=buf: s.mul(
                    ctxst_sb[:, buf * 512 + sub * 128:buf * 512 + (sub + 1) * 128],
                    O_ps[sub][:, 0:128], linv_sb[:, sub:sub + 1]), mw)
            fin_sc_box[0] = cnt["sc"]
            add("scalar", lambda s, p=p, h=h, buf=buf: s.dma_start(
                out=ctx_hbm[p * 512:(p + 1) * 512, h * 128:(h + 1) * 128]
                .rearrange("(s p) d -> p s d", p=128),
                in_=ctxst_sb[:, buf * 512:(buf + 1) * 512].rearrange("p (s d) -> p s d", s=4)),
                (("sc", fin_sc_box[0]),), csem)
            ctxst_store[buf] = (csem, cnt[csem])

        wff_pref = {}

        def emit_wff_prefetch(p, tt):
            cb = tt % 2
            for c in range(16):
                add("sync", lambda s, tt=tt, c=c, cb=cb: s.dma_start(
                    out=ctxT_sb[:, cb * 2048 + c * 128:cb * 2048 + (c + 1) * 128],
                    in_=ctx_hbm[tt * 128:(tt + 1) * 128, c * 128:(c + 1) * 128],
                    transpose=True),
                    ((ctx_dq_after_slot[p][0], ctx_dq_after_slot[p][1],
                      ("tn", ctxT_use[cb])) if c == 0 else ()), f"dCT{cb}")
            ctxT_dh = cnt[f"dCT{cb}"]
            xw = [("vc", h1t_free[0])]
            if tt - 1 >= 0:
                xw.append(("dSp", h1_spill_dh[tt - 1]))
            add("gpsimd", lambda s, tt=tt: s.dma_start(
                out=h1t_sb[:, :], in_=xs_d[tt * 128:(tt + 1) * 128, :]), xw, "dX")
            wff_pref[tt] = (ctxT_dh, cnt["dX"])

        def emit_wff_tile(p, tt):
            cb = tt % 2
            ctxT_dh, xld_dh = wff_pref[tt]
            vadd(lambda v: v.tensor_add(h1t_sb[:, :], h1t_sb[:, :], bo_sb[:, :]),
                 (("dX", xld_dh),))
            for e4 in range(4):
                ww = [(f"dCT{cb}", ctxT_dh), ("dWo", wo_dh), ("vc", woprev[e4 % 2]),
                      ("sc", max(exp_hist))]
                for c in range(16):
                    add("tensor", lambda t, e4=e4, c=c, cb=cb: t.matmul(
                        wo_ps[e4 % 2][:, :], ctxT_sb[:, cb * 2048 + c * 128:cb * 2048 + (c + 1) * 128],
                        wov[:, c, e4 * 512:(e4 + 1) * 512],
                        start=(c == 0), stop=(c == 15)),
                        ww if c == 0 else (), "tn" if c == 15 else None)
                wo_mm = cnt["tn"]
                sl = slice(e4 * 512, (e4 + 1) * 512)
                vadd(lambda v, e4=e4, sl=sl: v.scalar_tensor_tensor(
                    h1t_sb[:, sl], wo_ps[e4 % 2][:, :], 1.0, h1t_sb[:, sl],
                    ALU.mult, ALU.add), (("tn", wo_mm),))
                woprev[e4 % 2] = cnt["vc"]
            ctxT_use[cb] = cnt["tn"]
            hpre_vc = cnt["vc"]
            wo_evac_box[0] = cnt["vc"]
            rstd_sc = ln_stats_scalar(h1t_sb, (("vc", hpre_vc),))
            h1_vc = vadd(lambda v: v.tensor_scalar(h1t_sb[:, :], h1t_sb[:, :], st8[:, 2:3],
                                                   st8[:, 6:7], ALU.subtract, ALU.mult),
                         (("sc", rstd_sc),))
            add("gpsimd", lambda s, tt=tt: s.dma_start(
                out=h1_spill[tt * 128:(tt + 1) * 128, :], in_=h1t_sb[:, :]),
                (("vc", h1_vc),), "dSp")
            h1_spill_dh[tt] = cnt["dSp"]
            emit_ffront(tt, h1_vc)

        def emit_ffront(tt, h1_vc):
            rstd_sc = ln_stats_scalar(h1t_sb, (("vc", h1_vc),))
            vadd(lambda v: v.tensor_scalar(x_sb[:, :], h1t_sb[:, :], st8[:, 2:3], st8[:, 6:7],
                                           ALU.subtract, ALU.mult), (("sc", rstd_sc),))
            h1t_free[0] = cnt["vc"]
            vadd(lambda v: v.tensor_mul(x_sb[:, :], x_sb[:, :], gamma_sb[:, :]))
            xn_vc = vadd(lambda v: v.tensor_add(x_sb[:, :], x_sb[:, :], beta_sb[:, :]),
                         (("dS2", ssf_dh),))
            sg = sadd(lambda s: s.activation(scra[:, :], x_sb[:, :], AF.Sign), (("vc", xn_vc),))
            vadd(lambda v: v.tensor_mul(scra[:, :], scra[:, :], ssf_sb[:, :]), (("sc", sg),))
            a3 = scra[:, :].rearrange("p (t j) -> p t j", t=64)
            w = 16
            while w >= 1:
                vadd(lambda v, w=w: v.tensor_tensor(a3[:, :, 0:w], a3[:, :, 0:w],
                                                    a3[:, :, w:2 * w], ALU.add))
                w //= 2
            vadd(lambda v: v.tensor_copy(ts64_sb[:, :], a3[:, :, 0]))
            vadd(lambda v: v.tensor_copy(m64c[:, :], ts64_sb[:, :]))
            c4 = m64c[:, :].rearrange("p (c u) -> p c u", c=8)
            w = 4
            while w >= 1:
                vadd(lambda v, w=w: v.tensor_tensor(c4[:, :, 0:w], c4[:, :, 0:w],
                                                    c4[:, :, w:2 * w], ALU.add))
                w //= 2
            vadd(lambda v: v.tensor_copy(st8b[:, 0:8], c4[:, :, 0]))
            vadd(lambda v: v.reduce_max(st8[:, 0:1], st8b[:, 0:8], AX.X))
            vadd(lambda v: v.tensor_single_scalar(st8b[:, 0:8], st8b[:, 0:8], st8[:, 0:1],
                                                  ALU.is_equal))
            vadd(lambda v: v.tensor_scalar(m64a[:, 0:8], st8b[:, 0:8], BIGI, BIGI,
                                           ALU.mult, ALU.subtract))
            vadd(lambda v: v.tensor_mul(st8b[:, 0:8], st8b[:, 0:8], iota8_sb[:, :]))
            vadd(lambda v: v.tensor_sub(st8b[:, 0:8], st8b[:, 0:8], m64a[:, 0:8]))
            vadd(lambda v: v.tensor_reduce(st8[:, 1:2], st8b[:, 0:8], AX.X, ALU.min))
            vadd(lambda v: v.tensor_single_scalar(m64a[:, :], tdiv8_sb[:, :], st8[:, 1:2],
                                                  ALU.is_equal))
            vadd(lambda v: v.tensor_mul(m64b[:, :], ts64_sb[:, :], m64a[:, :]))
            vadd(lambda v: v.tensor_scalar(m64c[:, :], m64a[:, :], BIGI, BIGI,
                                           ALU.mult, ALU.subtract))
            vadd(lambda v: v.tensor_add(m64b[:, :], m64b[:, :], m64c[:, :]))
            vadd(lambda v: v.reduce_max(st8[:, 2:3], m64b[:, :], AX.X))
            vadd(lambda v: v.tensor_single_scalar(m64a[:, :], m64b[:, :], st8[:, 2:3],
                                                  ALU.is_equal))
            vadd(lambda v: v.tensor_scalar(m64c[:, :], m64a[:, :], BIGI, BIGI,
                                           ALU.mult, ALU.subtract))
            vadd(lambda v: v.tensor_mul(m64a[:, :], m64a[:, :], iota64_sb[:, :]))
            vadd(lambda v: v.tensor_sub(m64a[:, :], m64a[:, :], m64c[:, :]))
            tidx_vc = vadd(lambda v, tt=tt: v.tensor_reduce(tidx_sb[:, tt:tt + 1], m64a[:, :],
                                                            AX.X, ALU.min))
            vadd(lambda v, tt=tt: v.tensor_single_scalar(scrb[:, :], tdiv32_sb[:, :],
                                                         tidx_sb[:, tt:tt + 1], ALU.is_equal))
            vadd(lambda v: v.tensor_mul(scrb[:, :], scrb[:, :], x_sb[:, :]))
            w = 1024
            while w >= 32:
                vadd(lambda v, w=w: v.tensor_tensor(scrb[:, 0:w], scrb[:, 0:w],
                                                    scrb[:, w:2 * w], ALU.add))
                w //= 2
            vadd(lambda v, tt=tt: v.tensor_copy(xsl_sb[:, tt * 32:(tt + 1) * 32], scrb[:, 0:32]))
            vadd(lambda v, tt=tt: v.tensor_reduce(st8[:, 3:4], xsl_sb[:, tt * 32:(tt + 1) * 32],
                                                  AX.X, ALU.min))
            vadd(lambda v, tt=tt: v.reduce_max(st8[:, 4:5], xsl_sb[:, tt * 32:(tt + 1) * 32], AX.X))
            vadd(lambda v, tt=tt: v.tensor_single_scalar(m64a[:, :], iota64_sb[:, :],
                                                         tidx_sb[:, tt:tt + 1], ALU.is_equal))
            vadd(lambda v: v.tensor_single_scalar(m64b[:, :], m64a[:, :], st8[:, 3:4], ALU.mult))
            vadd(lambda v: v.tensor_scalar(m64c[:, :], m64a[:, :], BIGI, BIGI,
                                           ALU.mult, ALU.subtract))
            vadd(lambda v: v.tensor_sub(m64b[:, :], m64b[:, :], m64c[:, :]))
            vadd(lambda v: v.tensor_tensor(accmin_sb[:, :], accmin_sb[:, :], m64b[:, :], ALU.min),
                 (("vc", accinit_vc),))
            vadd(lambda v: v.tensor_single_scalar(m64b[:, :], m64a[:, :], st8[:, 4:5], ALU.mult))
            vadd(lambda v: v.tensor_add(m64b[:, :], m64b[:, :], m64c[:, :]))
            vadd(lambda v: v.tensor_tensor(accmax_sb[:, :], accmax_sb[:, :], m64b[:, :], ALU.max))
            add("gpsimd", lambda s, tt=tt: s.dma_start(
                out=tidx_hbm[tt:tt + 1, :].rearrange("a p -> p a"),
                in_=tidx_sb[:, tt:tt + 1]), (("vc", tidx_vc),), "dTx")
            tidx_exp_dh[tt] = cnt["dTx"]

        pending = None
        for p in range(4):
            for h in range(H):
                if h % 8 == 0:
                    half = h // 8
                    add("sync", lambda s, p=p, half=half: s.dma_start(
                        out=qt_sb[:, :].rearrange("p (h q) -> p h q", h=8),
                        in_=qT_scr[half * 1024:(half + 1) * 1024, p * 512:(p + 1) * 512]
                        .rearrange("(h d) q -> d h q", d=128)),
                        (("dqQ0", q_done[0]), ("dqQ1", q_done[1]),
                         ("tn", qt_last_use_box[0])), "dQt")
                    qt_dh = cnt["dQt"]
                if pending is not None and h % 4 == 0:
                    emit_wff_prefetch(pending, 4 * pending + h // 4)
                flash(p, h, qt_dh)
                qt_last_use_box[0] = cnt["tn"]
                if pending is not None and h % 4 == 3:
                    emit_wff_tile(pending, 4 * pending + h // 4)
            ctx_dq_after_slot[p] = ((f"dqC{p}0", cnt[f"dqC{p}0"]), (f"dqC{p}1", cnt[f"dqC{p}1"]))
            pending = p
        for tt in range(12, 16):
            emit_wff_prefetch(3, tt)
            emit_wff_tile(3, tt)

        tr1 = add("tensor", lambda t: t.transpose(sT_ps[0][0:64, 0:128], accmin_sb[:, :],
                                                  identf_sb[:, :]),
                  (("vc", cnt["vc"]), ("sc", fin_sc_box[0])), "tn")
        tr2 = add("tensor", lambda t: t.transpose(sT_ps[1][0:64, 0:128], accmax_sb[:, :],
                                                  identf_sb[:, :]), (), "tn")
        vadd(lambda v: v.tensor_reduce(mm2_sb[:, 0:1], sT_ps[0][0:64, 0:128], AX.X, ALU.min),
             (("tn", tr1),))
        vadd(lambda v: v.reduce_max(mm2_sb[:, 1:2], sT_ps[1][0:64, 0:128], AX.X), (("tn", tr2),))
        pack_vc = vadd(lambda v: v.tensor_scalar_mul(mm2_sb[:, 1:2], mm2_sb[:, 1:2], -1.0))
        add("gpsimd", lambda g: g.dma_start(out=ar_in_d, in_=mm2_sb[:, :]),
            (("vc", pack_vc),), "ds")
        arin_ds = cnt["ds"]
        add("gpsimd", lambda g: g.collective_compute(
            "AllReduce", ALU.min, replica_groups=[[0, 1, 2, 3, 4, 5, 6, 7]],
            ins=[ar_in_d], outs=[ar_out_d]), (("ds", arin_ds),), "cc")
        ar_cc = cnt["cc"]
        add("gpsimd", lambda g: g.dma_start(out=mm2_sb[:, :], in_=ar_out_d),
            (("cc", ar_cc),), "ds")
        arback_ds = cnt["ds"]
        vadd(lambda v: v.tensor_copy(tabs_sb[:, 1025:1026], mm2_sb[:, 0:1]), (("ds", arback_ds),))
        tabmm_vc = vadd(lambda v: v.tensor_scalar_mul(tabs_sb[:, 1026:1027], mm2_sb[:, 1:2], -1.0))

        psA.close()
        psB = ExitStack()
        tab_ps = psB.enter_context(nc.psum_tensor("tab_ps", [128, 1027], F32))

        ohT_use = 0
        tab_evac = 0
        tbufs = [scra, x_sb]
        tab_mm_done = [0] * (NTT + 1)
        store_done = [0] * (NTT + 1)
        h2_done = [0] * (NTT + 1)
        stats3_done = [0] * (NTT + 1)

        def fback_A(tt):
            nonlocal ohT_use, tab_evac
            tb = tbufs[tt % 2]
            add("sync", lambda s, tt=tt: s.dma_start(
                out=ohT_sb[:, :],
                in_=tidx_hbm[tt:tt + 1, :].broadcast_to([64, 128])),
                (("dTx", cnt["dTx"]), ("tn", tab_mm_done[tt - 1] if tt else 0),
                 ("vc", ohT_use)), "dOH")
            ohload_dh = cnt["dOH"]
            ohT_vc = vadd(lambda v: v.tensor_single_scalar(ohT_sb[:, :], ohT_sb[:, :],
                                                           iota64c_sb[:, :], ALU.is_equal),
                          (("dOH", ohload_dh),))
            ohT_use = ohT_vc
            tmw = [("vc", ohT_vc), ("vc", tabmm_vc), ("vc", tab_evac), ("sc", cnt["sc"]),
                   ("dh", store_done[tt - 2] if tt >= 2 else 0)]
            add("tensor", lambda t: t.matmul(tab_ps[:, 0:512], ohT_sb[:, :], tabs_sb[:, 0:512],
                                             start=True, stop=True), tmw, None)
            add("tensor", lambda t: t.matmul(tab_ps[:, 512:1024], ohT_sb[:, :],
                                             tabs_sb[:, 512:1024], start=True, stop=True), (), None)
            tab_mm_done[tt] = add("tensor", lambda t: t.matmul(
                tab_ps[:, 1024:1027], ohT_sb[:, :], tabs_sb[:, 1024:1027],
                start=True, stop=True), (), "tn")
            tab_evac = vadd(lambda v, tb=tb: v.tensor_copy(tb[:, 0:1027], tab_ps[:, :]),
                            (("tn", tab_mm_done[tt]),
                             ("dh", store_done[tt - 2] if tt >= 2 else 0)))

        fback_A(0)
        for tt in range(NTT):
            tb = tbufs[tt % 2]
            add("sync", lambda s, tt=tt: s.dma_start(
                out=h1t_sb[:, :], in_=h1_spill[tt * 128:(tt + 1) * 128, :]),
                (("dSp", cnt["dSp"]), ("vc", h2_done[tt - 1] if tt else h1t_free[0])), "dHR")
            h1re_dh = cnt["dHR"]
            vadd(lambda v, tb=tb: v.tensor_sub(st8[:, 0:1], tb[:, 1026:1027], tb[:, 1025:1026]),
                 (("vc", tab_evac),))
            vadd(lambda v: v.tensor_scalar_add(st8[:, 0:1], st8[:, 0:1], 1e-8))
            vadd(lambda v: v.reciprocal(st8[:, 1:2], st8[:, 0:1]))
            vadd(lambda v, tt=tt, tb=tb: v.tensor_scalar(
                scrb[:, 0:32], xsl_sb[:, tt * 32:(tt + 1) * 32],
                tb[:, 1025:1026], st8[:, 1:2], ALU.subtract, ALU.mult),
                (("sc", stats3_done[tt - 1] if tt else 0),))
            vadd(lambda v: v.tensor_scalar(scrb[:, 0:32], scrb[:, 0:32], 0.0, 16.0,
                                           ALU.max, ALU.mult))
            vadd(lambda v: v.tensor_scalar_min(scrb[:, 0:32], scrb[:, 0:32], UCLIP16))
            vadd(lambda v: v.tensor_copy(i32_sb[:, :], scrb[:, 0:32]))
            vadd(lambda v: v.tensor_copy(scrb[:, 32:64], i32_sb[:, :]))
            vadd(lambda v: v.tensor_tensor(scrb[:, 64:96], scrb[:, 32:64], scrb[:, 0:32],
                                           ALU.is_gt))
            vadd(lambda v: v.tensor_sub(scrb[:, 32:64], scrb[:, 32:64], scrb[:, 64:96]))
            vadd(lambda v: v.tensor_sub(scrb[:, 64:96], scrb[:, 0:32], scrb[:, 32:64]))
            tv3 = tb[:, 0:1024].rearrange("p (jj g) -> p jj g", g=16)
            tv4 = tb[:, 0:1024].rearrange("p (a j g) -> p a j g", a=2, j=32)
            gi_ap = scrb[:, 32:64]
            vadd(lambda v: v.tensor_single_scalar(m64a[:, 0:32], gi_ap, 8.0, ALU.is_ge))
            vadd(lambda v: v.scalar_tensor_tensor(m64a[:, 32:64], m64a[:, 0:32], -8.0, gi_ap,
                                                  ALU.mult, ALU.add))
            for k in (3, 2, 1, 0):
                w = 1 << k
                if k < 3:
                    vadd(lambda v, w=w: v.tensor_single_scalar(m64a[:, 0:32], m64a[:, 32:64],
                                                               float(w), ALU.is_ge))
                    if k > 0:
                        vadd(lambda v, w=w: v.scalar_tensor_tensor(
                            m64a[:, 32:64], m64a[:, 0:32], -float(w), m64a[:, 32:64],
                            ALU.mult, ALU.add))
                vadd(lambda v, w=w, tv3=tv3: v.tensor_tensor(
                    tv3[:, :, w:2 * w], tv3[:, :, w:2 * w], tv3[:, :, 0:w], ALU.subtract))
                for a in range(2):
                    vadd(lambda v, w=w, a=a, tv4=tv4: v.tensor_tensor(
                        tv4[:, a, :, w:2 * w], tv4[:, a, :, w:2 * w],
                        m64a[:, 0:32].unsqueeze(2).broadcast_to([128, 32, w]), ALU.mult))
                vadd(lambda v, w=w, tv3=tv3: v.tensor_tensor(
                    tv3[:, :, 0:w], tv3[:, :, 0:w], tv3[:, :, w:2 * w], ALU.add))
            vadd(lambda v, tv3=tv3: v.tensor_mul(m64c[:, 0:32], tv3[:, 32:64, 0],
                                                 scrb[:, 64:96]))
            vadd(lambda v, tv3=tv3: v.tensor_add(m64c[:, 0:32], m64c[:, 0:32],
                                                 tv3[:, 0:32, 0]))
            vadd(lambda v, tb=tb: v.tensor_single_scalar(m64c[:, 0:32], m64c[:, 0:32],
                                                         tb[:, 1024:1025], ALU.mult))
            if tt + 1 < NTT:
                fback_A(tt + 1)
            vadd(lambda v, tt=tt: v.tensor_single_scalar(scrb[:, :], tdiv32_sb[:, :],
                                                         tidx_sb[:, tt:tt + 1], ALU.is_equal))
            vadd(lambda v: v.tensor_tensor(
                scrb[:, :].rearrange("p (t j) -> p t j", t=64),
                scrb[:, :].rearrange("p (t j) -> p t j", t=64),
                m64c[:, 0:32].rearrange("p (a j) -> p a j", a=1).broadcast_to([128, 64, 32]),
                ALU.mult))
            h2_done[tt] = vadd(lambda v: v.scalar_tensor_tensor(
                scrb[:, :], h1t_sb[:, :], 2.0, scrb[:, :], ALU.mult, ALU.add),
                (("dHR", h1re_dh),))
            sadd(lambda s, tb=tb: s.activation(tb[:, :], scrb[:, :], AF.Identity,
                                               accum_out=st8[:, 0:1]), (("vc", h2_done[tt]),))
            sadd(lambda s, tb=tb: s.activation(tb[:, :], scrb[:, :], AF.Square,
                                               accum_out=st8[:, 1:2]))
            sadd(lambda s: s.mul(st8[:, 2:3], st8[:, 0:1], 1.0 / E))
            sadd(lambda s: s.mul(st8[:, 3:4], st8[:, 1:2], 1.0 / E))
            sadd(lambda s: s.activation(st8[:, 4:5], st8[:, 2:3], AF.Square))
            sadd(lambda s: s.mul(st8[:, 4:5], st8[:, 4:5], -1.0))
            stats3_done[tt] = sadd(lambda s: s.add(st8[:, 3:4], st8[:, 3:4], st8[:, 4:5]))
            sadd(lambda s: s.activation(st8[:, 5:6], st8[:, 3:4], AF.Ln, bias=eps_sb[:, 0:1]))
            rstd_sc = sadd(lambda s: s.activation(st8[:, 6:7], st8[:, 5:6], AF.Exp, scale=-0.5))
            outv = vadd(lambda v, tb=tb: v.tensor_scalar(tb[:, :], scrb[:, :], st8[:, 2:3],
                                                         st8[:, 6:7], ALU.subtract, ALU.mult),
                        (("sc", rstd_sc),))
            add("sync", lambda s, tt=tt, tb=tb: s.dma_start(
                out=out_d[tt * 128:(tt + 1) * 128, :], in_=tb[:, :]), (("vc", outv),), "dh")
            store_done[tt] = cnt["dh"]

        final_waits = [(k, cnt[k]) for k in semkeys if cnt[k] > 0 and k != "gc"]
        add("sync", lambda s: s.nop(), final_waits, None)

        for ek in ENGS:
            def make(ek):
                def body(eng):
                    last = {}
                    for waits, emitfn, inc, n in prog[ek]:
                        for skey, val in waits:
                            if val > 0 and last.get(skey, 0) < val:
                                eng.wait_ge(sem[skey], val)
                                last[skey] = val
                        inst = emitfn(eng)
                        if inc is not None:
                            inst.then_inc(sem[inc], n)
                return body
            getattr(block, ek)(make(ek))

        psB.close()
    return nc


def _host_consts():
    slopes = alibi = None
    sl = 2.0 ** (-8.0 * np.arange(1, H + 1) / H)
    tri = (np.arange(128)[:, None] <= np.arange(128)[None, :]).astype(ml_dtypes.bfloat16)
    tdiv32 = (np.arange(E) // 32).astype(ml_dtypes.bfloat16)[None, :]
    iota64 = np.arange(64, dtype=np.float32)[None, :]
    tdiv8 = (np.arange(64) // 8).astype(np.float32)[None, :]
    iota8 = np.arange(8, dtype=np.float32)[None, :]
    iota64c = np.arange(64, dtype=np.float32)[:, None]
    identf = np.eye(128, dtype=np.float32)
    identb = np.eye(128, dtype=ml_dtypes.bfloat16)
    return sl, tri, tdiv32, iota64, tdiv8, iota8, iota64c, identf, identb


def _alibik():
    sl = 2.0 ** (-8.0 * np.arange(1, H + 1) / H)
    kk = np.arange(128, dtype=np.float64)
    return (kk[:, None] * sl[None, :] * INV).astype(np.float32)


def _tilefac_table(r):
    sl = 2.0 ** (-8.0 * np.arange(1, H + 1) / H)
    tab = np.zeros((4 * H, 68), dtype=np.float32)
    for p in range(4):
        own_j = QB[r][p]
        q0 = 512 * own_j
        for h in range(H):
            row = p * H + h
            for slot in range(64):
                r2, pj, sub = slot // 16, (slot % 16) // 4, slot % 4
                j = QB[r2][pj]
                if j < own_j:
                    k0 = 512 * j + 128 * sub
                    tab[row, slot] = np.float32(np.exp(sl[h] * (k0 - q0) * INV))
            for dk in range(4):
                tab[row, 64 + dk] = np.float32(np.exp(sl[h] * (128 * dk) * INV))
    return tab


_NC_CACHE = {}


def make_in_maps(L0_attn_x, L0_attn_Wq, L0_attn_Wk, L0_attn_Wv, L0_attn_Wo, L0_attn_bo,
                 L0_ffn_ln_gamma, L0_ffn_ln_beta, L0_ffn_spline_bases, L0_ffn_spline_slopes,
                 L0_ffn_output_scale):
    x = np.ascontiguousarray(np.asarray(L0_attn_x, dtype=np.float32))
    wq = np.ascontiguousarray(np.asarray(L0_attn_Wq, dtype=np.float32))
    wk = np.ascontiguousarray(np.asarray(L0_attn_Wk, dtype=np.float32))
    wv = np.ascontiguousarray(np.asarray(L0_attn_Wv, dtype=np.float32))
    wo = np.ascontiguousarray(np.asarray(L0_attn_Wo, dtype=np.float32))
    bo = np.asarray(L0_attn_bo, dtype=np.float32)[None, :]
    gamma = np.asarray(L0_ffn_ln_gamma, dtype=np.float32)[None, :]
    beta = np.asarray(L0_ffn_ln_beta, dtype=np.float32)[None, :]
    bases = np.ascontiguousarray(np.asarray(L0_ffn_spline_bases, dtype=np.float32).reshape(64, 512))
    slopes = np.ascontiguousarray(np.asarray(L0_ffn_spline_slopes, dtype=np.float32).reshape(64, 512))
    oscale = np.asarray(L0_ffn_output_scale, dtype=np.float32)[:, None]

    slv, tri, tdiv32, iota64, tdiv8, iota8, iota64c, identf, identb = _host_consts()

    in_maps = []
    for core in range(8):
        b = core // 4
        r = core % 4
        xs = np.ascontiguousarray(
            np.concatenate([x[b, 512 * j:512 * (j + 1), :] for j in QB[r]], axis=0))
        in_maps.append({
            "xs": xs, "wq": wq, "wk": wk, "wv": wv, "wo": wo, "bo": bo,
            "gamma": gamma, "beta": beta, "bases": bases, "slopes": slopes,
            "oscale": oscale, "alibik": _alibik(), "tilefac": _tilefac_table(r), "tri": tri,
            "tdiv32": tdiv32, "iota64": iota64, "tdiv8": tdiv8, "iota8": iota8,
            "iota64c": iota64c, "identf": identf, "identb": identb,
        })

    return in_maps


def assemble_out(core_outs):
    out = np.empty((2, S, E), dtype=np.float32)
    for core in range(8):
        b = core // 4
        r = core % 4
        o = core_outs[core]
        for p, j in enumerate(QB[r]):
            out[b, 512 * j:512 * (j + 1), :] = o[512 * p:512 * (p + 1), :]
    return out


def kernel(**inputs):
    if "nc" not in _NC_CACHE:
        _NC_CACHE["nc"] = build_nc()
    nc = _NC_CACHE["nc"]
    in_maps = make_in_maps(**inputs)
    res = run_bass_kernel_spmd(nc, in_maps, core_ids=list(range(8)))
    return assemble_out([res.results[c]["out"] for c in range(8)])
